# version 18
# baseline (speedup 1.0000x reference)
import numpy as np
import ml_dtypes
from contextlib import ExitStack
import concourse.bass as bass
import concourse.mybir as mybir
from concourse.bass_utils import run_bass_kernel_spmd

F32 = mybir.dt.float32
BF16 = mybir.dt.bfloat16
AF = mybir.ActivationFunctionType
ALU = mybir.AluOpType
AX = mybir.AxisListType
BF = ml_dtypes.bfloat16

T = 2048
D = 1024
H = 16
DH = 64
CB = 8
NKR, NQR = 5, 2
NAUG = NKR + NQR
KD = DH + NAUG
SCALE = 0.125
EPS = 1e-5
LAMBDA_INIT = 0.8


def _bf16_splits(v, n):
    parts, r = [], np.asarray(v, np.float64).copy()
    for _ in range(n):
        p = r.astype(BF).astype(np.float64)
        parts.append(p)
        r = r - p
    return np.stack(parts)


def _slopes():
    return 2.0 ** (-8.0 * (np.arange(1, H + 1, dtype=np.float64)) / H)


TAU = 36.0
DTAB = [min(16, int((np.ceil(TAU / s) + 127) // 128) + 1) for s in _slopes()]


class Bld:
    ENGS = ("sync", "gpsimd", "tensor", "scalar", "vector")

    def __init__(self, nc, stack):
        self.nc = nc
        self.prog = {e: [] for e in self.ENGS}
        self.ev = {}
        for name in ("T", "S", "V", "P", "DX", "DC", "DXB", "DW",
                     "DS0", "DS1", "DKA", "DQA", "DD", "DL"):
            sem = stack.enter_context(nc.semaphore("ev" + name))
            self.ev[name] = [sem, 0]

    def wait(self, eng, *marks):
        seen = {}
        for m in marks:
            if isinstance(m, str):
                name, k = m, self.ev[m][1]
            else:
                name, k = m
            if k > seen.get(name, 0):
                seen[name] = k
        for name, k in seen.items():
            sem = self.ev[name][0]
            self.prog[eng].append(lambda e, s=sem, kk=k: e.wait_ge(s, kk))

    def op(self, eng, fn, inc=None, n=1):
        if inc is None:
            self.prog[eng].append(fn)
        else:
            rec = self.ev[inc]
            rec[1] += n
            self.prog[eng].append(
                lambda e, fn=fn, s=rec[0], nn=n: fn(e).then_inc(s, nn))

    def mark(self, name):
        return (name, self.ev[name][1])

    def emit(self, block):
        for eng in self.ENGS:
            ops = self.prog[eng]
            if ops:
                getattr(block, eng)(
                    lambda engine, ops=ops: [f(engine) for f in ops])


_DUMP = None


def build_nc():
    nc = bass.Bass("TRN2", target_bir_lowering=False)
    dt = nc.dram_tensor
    x_d = dt("x", [D, T], F32, kind="ExternalInput").ap()
    y_d = dt("y", [D, T], F32, kind="ExternalOutput").ap()
    wq0_d = dt("wq0", [8, D, 128], F32, kind="ExternalInput").ap()
    wq1_d = dt("wq1", [8, D, 128], F32, kind="ExternalInput").ap()
    wk0_d = dt("wk0", [8, D, 128], F32, kind="ExternalInput").ap()
    wk1_d = dt("wk1", [8, D, 128], F32, kind="ExternalInput").ap()
    wv_d = dt("wv", [8, D, 128], F32, kind="ExternalInput").ap()
    wo_d = dt("wo", [D, D], F32, kind="ExternalInput").ap()
    wrt_d = dt("wrt", [D, D], F32, kind="ExternalInput").ap()
    wkt_d = dt("wkt", [D, D], F32, kind="ExternalInput").ap()
    wvt_d = dt("wvt", [D, D], F32, kind="ExternalInput").ap()
    mur_d = dt("mur", [128, 8], F32, kind="ExternalInput").ap()
    muk_d = dt("muk", [128, 8], F32, kind="ExternalInput").ap()
    lnw_d = dt("lnw", [128, 8], F32, kind="ExternalInput").ap()
    lnb_d = dt("lnb", [128, 8], F32, kind="ExternalInput").ap()
    lv_d = dt("lvec", [64, 4], F32, kind="ExternalInput").ap()
    iden_d = dt("iden", [128, 128], BF16, kind="ExternalInput").ap()
    mask_d = dt("maskt", [128, 128], BF16, kind="ExternalInput").ap()
    onesc_d = dt("onesc", [128, 1], BF16, kind="ExternalInput").ap()
    onesf_d = dt("onesf", [65, 128], F32, kind="ExternalInput").ap()
    kaug_d = dt("kaug", [8, 7, 8192], BF16, kind="ExternalInput").ap()
    qaug_d = dt("qaug", [8, 2, 7, 4096], BF16, kind="ExternalInput").ap()

    with ExitStack() as st:
        e = st.enter_context
        R = e(nc.sbuf_tensor("R", [128, 16384], F32))
        XLN = e(nc.sbuf_tensor("XLN", [128, 16384], BF16))
        A2 = e(nc.sbuf_tensor("A2", [128, 16384], BF16))
        WARE = e(nc.sbuf_tensor("WARE", [128, 24576], BF16))
        VTM = e(nc.sbuf_tensor("VTM", [128, 2080], BF16))
        PT00 = e(nc.sbuf_tensor("PT00", [128, 1024], BF16))
        PT01 = e(nc.sbuf_tensor("PT01", [128, 1024], BF16))
        PT10 = e(nc.sbuf_tensor("PT10", [128, 1024], BF16))
        PT11 = e(nc.sbuf_tensor("PT11", [128, 1024], BF16))
        ROWS = e(nc.sbuf_tensor("ROWS", [128, 2048], F32))
        TMP = e(nc.sbuf_tensor("TMP", [128, 2048], F32))
        SQS = e(nc.sbuf_tensor("SQS", [128, 1024], BF16))
        RECIPS = e(nc.sbuf_tensor("RECIPS", [128, 32], F32))
        MUVAR = e(nc.sbuf_tensor("MUVAR", [128, 32], F32))
        BND = e(nc.sbuf_tensor("BND", [128, 8], BF16))
        IDEN = e(nc.sbuf_tensor("IDEN", [128, 128], BF16))
        MASKT = e(nc.sbuf_tensor("MASKT", [128, 128], BF16))
        ONESC = e(nc.sbuf_tensor("ONESC", [128, 1], BF16))
        ONESF = e(nc.sbuf_tensor("ONESF", [65, 128], F32))
        MURC = e(nc.sbuf_tensor("MURC", [128, 8], F32))
        MUKC = e(nc.sbuf_tensor("MUKC", [128, 8], F32))
        MURB = e(nc.sbuf_tensor("MURB", [128, 8], F32))
        MUKB = e(nc.sbuf_tensor("MUKB", [128, 8], F32))
        LNWC = e(nc.sbuf_tensor("LNWC", [128, 8], F32))
        LNBC = e(nc.sbuf_tensor("LNBC", [128, 8], BF16))
        LVEC = e(nc.sbuf_tensor("LVEC", [64, 4], F32))
        NLAM = e(nc.sbuf_tensor("NLAM", [128, 1], F32))
        BIASD = e(nc.sbuf_tensor("BIASD", [128, 8], F32))
        SC1 = e(nc.sbuf_tensor("SC1", [1, 8], F32))
        EPSC = e(nc.sbuf_tensor("EPSC", [128, 1], F32))
        PW1 = e(nc.psum_tensor("PW1", [128, 1024], F32))
        PW2 = e(nc.psum_tensor("PW2", [128, 1024], F32))
        PO1 = e(nc.psum_tensor("PO1", [128, 512], F32))
        PO2 = e(nc.psum_tensor("PO2", [128, 512], F32))
        PR1 = e(nc.psum_tensor("PR1", [128, 512], F32))
        PTR = e(nc.psum_tensor("PTR", [128, 512], BF16))
        b = Bld(nc, st)
        PW = (PW1, PW2)
        PO = (PO1, PO2)

        pw_free = [[], []]
        po_free = []

        WK_OFF, WV_OFF, WR_OFF = 0, 8192, 16384

        def qtile(j, hh):
            return WARE[0:KD, (j + 2 * hh) * 1024:(j + 2 * hh) * 1024 + 1024]

        def ktile(j, hh):
            off = 4096 + (j + 2 * hh) * 2048
            return WARE[0:KD, off:off + 2048]

        SLC = (12288, 17408)
        WO_OFF = 12288

        def slice_view(hp, widx, cbk):
            base = SLC[hp % 2] + widx * 1024
            return WARE[:, base + cbk * 128: base + cbk * 128 + 128]

        x_r = x_d.rearrange("(cb p) t -> p cb t", p=128)
        b.op("sync", lambda e: e.dma_start(
            out=R[:].rearrange("p (cb t) -> p cb t", cb=CB), in_=x_r),
            inc="DX", n=16)
        ds_x = b.mark("DX")
        for (dst, src) in ((IDEN, iden_d), (MASKT, mask_d), (ONESC, onesc_d),
                           (ONESF, onesf_d), (MURC, mur_d), (MUKC, muk_d),
                           (LNWC, lnw_d), (LVEC, lv_d)):
            b.op("sync", lambda e, d=dst, s=src: e.dma_start(out=d[:], in_=s),
                 inc="DC", n=16)
        ds_consts = b.mark("DC")
        b.op("gpsimd", lambda e: e.dma_start(out=LNBC[:], in_=lnb_d),
             inc="DL", n=16)
        dg_lnb = b.mark("DL")
        b.op("gpsimd", lambda e: e.dma_start(
            out=XLN[:].rearrange("p (cb t) -> p cb t", cb=CB), in_=x_r),
            inc="DXB", n=16)
        dg_xbf = b.mark("DXB")
        for off, src in ((WK_OFF, wkt_d), (WV_OFF, wvt_d), (WR_OFF, wrt_d)):
            b.op("gpsimd", lambda e, o=off, s=src: e.dma_start(
                out=WARE[:, o:o + 8192].rearrange("p (cb o) -> p cb o", cb=CB),
                in_=s.rearrange("(cb p) o -> p cb o", p=128)), inc="DW", n=16)
        dg_ffnw1 = b.mark("DW")
        b.wait("gpsimd", ds_consts)
        b.op("gpsimd", lambda e: e.memset(
            bass.AP(VTM, 64, [[2080, 128], [65, 32]]), 1.0), inc="P")
        b.op("gpsimd", lambda e: e.memset(EPSC[:], EPS), inc="P")
        p_vones = b.mark("P")
        b.wait("vector", ds_consts)
        b.op("vector", lambda e: e.tensor_scalar(
            MURB[:], MURC[:], -1.0, 1.0, ALU.mult, ALU.add), inc="V")
        b.op("vector", lambda e: e.tensor_scalar(
            MUKB[:], MUKC[:], -1.0, 1.0, ALU.mult, ALU.add), inc="V")
        v_params = b.mark("V")

        def emit_ln(xln_ready, r_marks):
            sq_free = [b.mark("T"), b.mark("T")]
            b.wait("vector", *xln_ready)
            b.wait("tensor", *xln_ready, ds_consts,
                   *pw_free[0], *pw_free[1], *po_free)
            for g in range(4):
                for cb in range(CB):
                    par = cb % 2
                    xv = XLN[:, cb * 2048 + g * 512: cb * 2048 + g * 512 + 512]
                    b.wait("vector", sq_free[par])
                    b.op("vector", lambda e, xv=xv, par=par: e.tensor_mul(
                        SQS[:, par * 512:par * 512 + 512], xv, xv), inc="V")
                    vmk = b.mark("V")
                    b.wait("tensor", vmk)
                    b.op("tensor", lambda e, xv=xv, cb=cb: e.matmul(
                        PR1[0:1, 0:512], ONESC[:], xv,
                        start=(cb == 0), stop=(cb == 7)), inc="T")
                    b.op("tensor", lambda e, par=par, cb=cb: e.matmul(
                        PO1[0:1, 0:512], ONESC[:],
                        SQS[:, par * 512:par * 512 + 512],
                        start=(cb == 0), stop=(cb == 7)), inc="T")
                    sq_free[par] = b.mark("T")
                tmk = b.mark("T")
                b.wait("vector", tmk)
                b.op("vector", lambda e, g=g: e.tensor_copy(
                    ROWS[0:1, g * 512:g * 512 + 512], PR1[0:1, 0:512]), inc="V")
                b.op("vector", lambda e, g=g: e.tensor_copy(
                    ROWS[32:33, g * 512:g * 512 + 512], PO1[0:1, 0:512]), inc="V")
                b.wait("tensor", b.mark("V"))
            def sev(fn):
                b.op("vector", fn, inc="V")
                b.wait("vector", b.mark("V"))

            def ses(fn):
                b.op("scalar", fn, inc="S")
                b.wait("scalar", b.mark("S"))
            b.wait("scalar", b.mark("V"), ("P", 2))
            b.wait("vector", b.mark("V"))
            ses(lambda e: e.activation(
                TMP[32:33, :], ROWS[0:1, :], AF.Square, scale=1.0 / D))
            smsq = b.mark("S")
            b.wait("vector", smsq)
            sev(lambda e: e.tensor_scalar_mul(ROWS[0:1, :], ROWS[0:1, :], 1.0 / D))
            sev(lambda e: e.scalar_tensor_tensor(
                ROWS[32:33, :], ROWS[32:33, :], 1.0 / D, TMP[32:33, :],
                ALU.mult, ALU.subtract))
            vrow = b.mark("V")
            b.wait("scalar", vrow)
            ses(lambda e: e.activation(
                ROWS[64:65, :], ROWS[32:33, :], AF.Ln, bias=EPSC[0:1, :],
                scale=1.0))
            ses(lambda e: e.activation(
                ROWS[32:33, :], ROWS[64:65, :], AF.Exp, scale=-0.5))
            srow = b.mark("S")
            b.wait("tensor", srow, vrow)
            b.wait("vector", *r_marks)
            for g in range(4):
                b.op("tensor", lambda e, g=g: e.matmul(
                    PO1[:, 0:512], ONESF[0:1, :],
                    ROWS[0:1, g * 512:g * 512 + 512], start=True, stop=True),
                    inc="T")
                b.op("tensor", lambda e, g=g: e.matmul(
                    PO2[:, 0:512], ONESF[32:33, :],
                    ROWS[32:33, g * 512:g * 512 + 512], start=True, stop=True),
                    inc="T")
                tmk = b.mark("T")
                b.wait("vector", tmk)
                for cb in range(CB):
                    par = cb % 2
                    off = cb * 2048 + g * 512
                    b.op("vector", lambda e, off=off, par=par: e.tensor_sub(
                        TMP[:, par * 512:par * 512 + 512],
                        R[:, off:off + 512], PO1[:, 0:512]), inc="V")
                    b.wait("vector", b.mark("V"))
                    b.op("vector", lambda e, off=off, par=par: e.tensor_mul(
                        XLN[:, off:off + 512],
                        TMP[:, par * 512:par * 512 + 512], PO2[:, 0:512]),
                        inc="V")
                vmk = b.mark("V")
                b.wait("tensor", vmk)
            return b.mark("V")

        def emit_ffn(dg_w_mark, xln_mark, a2_free):
            nonlocal pw_free
            last_v = xln_mark
            for qh in range(2):
                b.wait("vector", xln_mark, a2_free, v_params, "S")
                stt_done = [None, None]
                mi = 0
                for (ydst, ac, bcol) in ((0, MUKC, MUKB), (8192, MURC, MURB)):
                    for cb in range(CB):
                        sl = mi % 2
                        mi += 1
                        to = sl * 1024
                        xc = XLN[:, cb * 2048 + qh * 1024:
                                 cb * 2048 + qh * 1024 + 1024]
                        yv = A2[:, ydst + cb * 1024: ydst + cb * 1024 + 1024]
                        if stt_done[sl] is not None:
                            b.wait("vector", stt_done[sl])
                        b.op("vector", lambda e, xc=xc, bcol=bcol, cb=cb, to=to:
                             e.tensor_scalar_mul(TMP[:, to:to + 1023],
                                                 xc[:, 0:1023],
                                                 bcol[:, cb:cb + 1]), inc="V")
                        if qh == 1:
                            b.op("vector", lambda e, bcol=bcol, cb=cb, to=to:
                                 e.tensor_scalar_mul(TMP[:, to + 1023:to + 1024],
                                                     BND[:, cb:cb + 1],
                                                     bcol[:, cb:cb + 1]), inc="V")
                        b.wait("vector", b.mark("V"))
                        b.op("vector", lambda e, xc=xc, ac=ac, cb=cb, yv=yv,
                             to=to: e.scalar_tensor_tensor(
                                 yv[:, 1:1024], xc[:, 1:1024], ac[:, cb:cb + 1],
                                 TMP[:, to:to + 1023], ALU.mult, ALU.add),
                             inc="V")
                        if qh == 0:
                            b.op("vector", lambda e, xc=xc, ac=ac, cb=cb, yv=yv:
                                 e.tensor_scalar_mul(yv[:, 0:1], xc[:, 0:1],
                                                     ac[:, cb:cb + 1]), inc="V")
                        else:
                            b.op("vector", lambda e, xc=xc, ac=ac, cb=cb, yv=yv,
                                 to=to: e.scalar_tensor_tensor(
                                     yv[:, 0:1], xc[:, 0:1], ac[:, cb:cb + 1],
                                     TMP[:, to + 1023:to + 1024],
                                     ALU.mult, ALU.add), inc="V")
                        stt_done[sl] = b.mark("V")
                if qh == 0:
                    b.op("vector", lambda e: e.tensor_copy(
                        BND[:], bass.AP(XLN, 1023, [[16384, 128], [2048, 8]])),
                        inc="V")
                    b.wait("vector", b.mark("V"))
                ymk = b.mark("V")

                b.wait("tensor", ymk, dg_w_mark, *pw_free[0], *pw_free[1])
                relu_mark, sq_mark = {}, {}
                for ob in range(CB):
                    par = ob % 2
                    if ob >= 2:
                        b.wait("tensor", relu_mark[ob - 2])
                    for cb in range(CB):
                        for nch in range(2):
                            b.op("tensor", lambda e, ob=ob, cb=cb, nch=nch,
                                 par=par, qh=qh: e.matmul(
                                     PW[par][:, nch * 512:nch * 512 + 512],
                                     WARE[:, WK_OFF + cb * 1024 + ob * 128:
                                          WK_OFF + cb * 1024 + ob * 128 + 128],
                                     A2[:, cb * 1024 + nch * 512:
                                        cb * 1024 + nch * 512 + 512],
                                     start=(cb == 0), stop=(cb == 7)), inc="T")
                    tmk = b.mark("T")
                    b.wait("scalar", tmk)
                    if ob >= 2:
                        b.wait("scalar", sq_mark[ob - 2])
                    b.op("scalar", lambda e, par=par: e.activation(
                        TMP[:, par * 1024:par * 1024 + 1024],
                        PW[par][:, 0:1024], AF.Relu), inc="S")
                    relu_mark[ob] = b.mark("S")
                    b.wait("vector", relu_mark[ob])
                    b.op("vector", lambda e, ob=ob, par=par, qh=qh: e.tensor_mul(
                        XLN[:, ob * 2048 + qh * 1024:
                            ob * 2048 + qh * 1024 + 1024],
                        TMP[:, par * 1024:par * 1024 + 1024],
                        TMP[:, par * 1024:par * 1024 + 1024]), inc="V")
                    sq_mark[ob] = b.mark("V")
                kact_mark = b.mark("V")

                b.wait("tensor", kact_mark, relu_mark[6], relu_mark[7])
                prod_mark, sig_mark = {}, {}
                for u in range(2 * CB):
                    ob, nch = u // 2, u % 2
                    c0 = qh * 1024 + nch * 512
                    if u >= 2:
                        b.wait("tensor", prod_mark[u - 2])
                    for cb in range(CB):
                        b.op("tensor", lambda e, ob=ob, cb=cb, nch=nch, c0=c0:
                             e.matmul(
                                 PW1[:, nch * 512:nch * 512 + 512],
                                 WARE[:, WV_OFF + cb * 1024 + ob * 128:
                                      WV_OFF + cb * 1024 + ob * 128 + 128],
                                 XLN[:, cb * 2048 + c0:cb * 2048 + c0 + 512],
                                 start=(cb == 0), stop=(cb == 7)), inc="T")
                    for cb in range(CB):
                        b.op("tensor", lambda e, ob=ob, cb=cb, nch=nch:
                             e.matmul(
                                 PW2[:, nch * 512:nch * 512 + 512],
                                 WARE[:, WR_OFF + cb * 1024 + ob * 128:
                                      WR_OFF + cb * 1024 + ob * 128 + 128],
                                 A2[:, 8192 + cb * 1024 + nch * 512:
                                    8192 + cb * 1024 + nch * 512 + 512],
                                 start=(cb == 0), stop=(cb == 7)), inc="T")
                    tmk = b.mark("T")
                    b.wait("scalar", tmk)
                    if u >= 2:
                        b.wait("scalar", prod_mark[u - 2])
                    b.op("scalar", lambda e, nch=nch: e.activation(
                        TMP[:, nch * 512:nch * 512 + 512],
                        PW2[:, nch * 512:nch * 512 + 512], AF.Sigmoid),
                        inc="S")
                    sig_mark[u] = b.mark("S")
                    b.wait("vector", sig_mark[u])
                    b.op("vector", lambda e, nch=nch: e.tensor_mul(
                        TMP[:, 1024 + nch * 512:1024 + nch * 512 + 512],
                        TMP[:, nch * 512:nch * 512 + 512],
                        PW1[:, nch * 512:nch * 512 + 512]), inc="V")
                    b.wait("vector", b.mark("V"))
                    b.op("vector", lambda e, ob=ob, nch=nch, c0=c0:
                         e.scalar_tensor_tensor(
                             R[:, ob * 2048 + c0:ob * 2048 + c0 + 512],
                             TMP[:, 1024 + nch * 512:1024 + nch * 512 + 512],
                             0.5,
                             R[:, ob * 2048 + c0:ob * 2048 + c0 + 512],
                             ALU.mult, ALU.add), inc="V")
                    prod_mark[u] = b.mark("V")
                pw_free[0] = [sig_mark[15], prod_mark[15]]
                pw_free[1] = [sig_mark[15]]
                last_v = b.mark("V")
            return last_v

        def finish_dump(mark, src_ap, cast):
            eng = "gpsimd" if cast else "sync"
            b.wait(eng, mark)
            b.op(eng, lambda e: e.dma_start(
                out=y_d.rearrange("(cb p) t -> p cb t", p=128),
                in_=src_ap.rearrange("p (cb t) -> p cb t", cb=CB)),
                inc="DD", n=16)
            b.wait(eng, "DD")
            with nc.Block() as block:
                b.emit(block)

        xln1 = emit_ln([dg_xbf], [ds_x])
        if _DUMP == "xln1":
            finish_dump(xln1, XLN[:], True)
            return nc
        ffn1 = emit_ffn(dg_ffnw1, xln1, ("T", 0))
        if _DUMP == "ffn1":
            finish_dump(ffn1, R[:], False)
            return nc

        def emit_cast(r_mark, xln_war):
            b.wait("scalar", r_mark, *xln_war)
            b.wait("vector", r_mark, *xln_war)
            for cb in range(CB):
                if cb % 2 == 0:
                    b.op("scalar", lambda e, cb=cb: e.copy(
                        XLN[:, cb * 2048:cb * 2048 + 2048],
                        R[:, cb * 2048:cb * 2048 + 2048]), inc="S")
                else:
                    b.op("vector", lambda e, cb=cb: e.tensor_copy(
                        XLN[:, cb * 2048:cb * 2048 + 2048],
                        R[:, cb * 2048:cb * 2048 + 2048]), inc="V")
            return [b.mark("S"), b.mark("V")]

        cast2 = emit_cast(ffn1, [b.mark("T")])
        xln2 = emit_ln(cast2, [ffn1])
        if _DUMP == "xln2":
            finish_dump(xln2, XLN[:], True)
            return nc

        b.wait("tensor", *po_free)
        b.op("tensor", lambda e: e.matmul(
            PR1[0:1, 0:1], LVEC[:, 0:1], LVEC[:, 1:2], start=True, stop=False),
            inc="T")
        b.op("tensor", lambda e: e.matmul(
            PR1[0:1, 1:2], LVEC[:, 2:3], LVEC[:, 3:4], start=False, stop=True),
            inc="T")
        tlam = b.mark("T")
        b.wait("scalar", tlam)
        b.op("scalar", lambda e: e.activation(
            SC1[0:1, 0:2], PR1[0:1, 0:2], AF.Exp), inc="S")
        slam = b.mark("S")
        b.wait("vector", slam)
        b.op("vector", lambda e: e.reduce_sum(
            SC1[0:1, 2:3], SC1[0:1, 0:2], axis=AX.X), inc="V")
        b.wait("vector", b.mark("V"))
        b.op("vector", lambda e: e.tensor_scalar_add(
            SC1[0:1, 3:4], SC1[0:1, 2:3], -2.0 * LAMBDA_INIT), inc="V")
        vlam = b.mark("V")
        b.wait("tensor", vlam)
        b.op("tensor", lambda e: e.matmul(
            PO1[:, 0:1], ONESF[0:1, :], SC1[0:1, 3:4], start=True, stop=True),
            inc="T")
        tlam2 = b.mark("T")
        b.wait("vector", tlam2)
        b.op("vector", lambda e: e.tensor_copy(NLAM[:], PO1[:, 0:1]), inc="V")
        po_free = [b.mark("V")]

        wsrc = (wq0_d, wq1_d, wk0_d, wk1_d, wv_d)
        slice_war = [[], []]

        def emit_slice_dma(hp):
            ch = "DS0" if hp % 2 == 0 else "DS1"
            b.wait("gpsimd", *slice_war[hp % 2],
                   *([ffn1] if hp <= 1 else []))
            for widx in range(5):
                base = SLC[hp % 2] + widx * 1024
                b.op("gpsimd", lambda e, base=base, w=wsrc[widx], hp=hp:
                     e.dma_start(
                         out=WARE[:, base:base + 1024].rearrange(
                             "p (cb o) -> p cb o", cb=CB),
                         in_=w[hp].rearrange("(cb p) o -> p cb o", p=128)),
                     inc=ch, n=16)
            return b.mark(ch)

        slice_ready = {0: emit_slice_dma(0)}
        qk_war = []
        vtm_war = [p_vones]
        exp_free = [b.mark("S"), b.mark("S")]
        pt_free = {(j, p): b.mark("T") for j in range(2) for p in range(2)}
        PTJ = ((PT00, PT01), (PT10, PT11))

        b.wait("tensor", xln2)
        for hp in range(8):
            if hp + 1 < 8:
                slice_ready[hp + 1] = emit_slice_dma(hp + 1)
            b.wait("sync", *qk_war, *([ffn1] if hp == 0 else []))
            b.op("sync", lambda e, hp=hp: e.dma_start(
                out=WARE[64:KD, 4096:12288], in_=kaug_d[hp]), inc="DKA", n=16)
            kaug_mark = b.mark("DKA")

            b.wait("tensor", slice_ready[hp], *qk_war,
                   *exp_free, *pw_free[0], *pw_free[1])
            kevac = []
            for widx in (2, 3):
                j = widx - 2
                for half in range(2):
                    for cb in range(CB):
                        for nch in range(2):
                            b.op("tensor", lambda e, widx=widx, cb=cb, hp=hp,
                                 half=half, nch=nch: e.matmul(
                                     PW[half][:, nch * 512:nch * 512 + 512],
                                     slice_view(hp, widx, cb),
                                     XLN[:, cb * 2048 + half * 1024 + nch * 512:
                                         cb * 2048 + half * 1024 + nch * 512 + 512],
                                     start=(cb == 0), stop=(cb == 7)), inc="T")
                tmk = b.mark("T")
                for hh in range(2):
                    for half in range(2):
                        eng = "vector"
                        b.wait(eng, tmk)
                        dst = ktile(j, hh)[0:64, half * 1024:half * 1024 + 1024]
                        src = PW[half][64 * hh:64 * hh + 64, 0:1024]
                        if eng == "vector":
                            b.op("vector", lambda e, d=dst, s=src:
                                 e.tensor_copy(d, s), inc="V")
                        else:
                            b.op("scalar", lambda e, d=dst, s=src:
                                 e.copy(d, s), inc="S")
                kevac += [b.mark("V"), b.mark("S")]
                b.wait("tensor", *kevac)
            b.wait("tensor", *vtm_war)
            vp_evac = {}
            for tt in range(16):
                par = tt % 2
                if tt >= 2:
                    b.wait("tensor", vp_evac[tt - 2])
                for cb in range(CB):
                    b.op("tensor", lambda e, tt=tt, cb=cb, hp=hp, par=par:
                         e.matmul(
                             PW[par][:, 0:128],
                             XLN[:, cb * 2048 + tt * 128:
                                 cb * 2048 + tt * 128 + 128],
                             slice_view(hp, 4, cb),
                             start=(cb == 0), stop=(cb == 7)), inc="T")
                tmk = b.mark("T")
                b.wait("vector", tmk)
                b.op("vector", lambda e, tt=tt, par=par: e.tensor_copy(
                    bass.AP(VTM, tt * 130, [[2080, 128], [65, 2], [1, 64]]),
                    PW[par][:, 0:128].rearrange("p (h c) -> p h c", h=2)),
                    inc="V")
                vp_evac[tt] = b.mark("V")
            vevac = [vp_evac[14], vp_evac[15]]

            for qh in range(2):
                b.wait("sync", *qk_war, *([ffn1] if hp == 0 else []))
                b.op("sync", lambda e, hp=hp, qh=qh: e.dma_start(
                    out=WARE[64:KD, 0:4096], in_=qaug_d[hp, qh]),
                    inc="DQA", n=16)
                qaug_mark = b.mark("DQA")
                b.wait("tensor", *qk_war, *exp_free, *vevac)
                for widx in (0, 1):
                    for cb in range(CB):
                        for nch in range(2):
                            b.op("tensor", lambda e, widx=widx, cb=cb, hp=hp,
                                 qh=qh, nch=nch: e.matmul(
                                     PW[widx][:, nch * 512:nch * 512 + 512],
                                     slice_view(hp, widx, cb),
                                     XLN[:, cb * 2048 + qh * 1024 + nch * 512:
                                         cb * 2048 + qh * 1024 + nch * 512 + 512],
                                     start=(cb == 0), stop=(cb == 7)), inc="T")
                tmk = b.mark("T")
                qevac = []
                for widx in (0, 1):
                    for hh in range(2):
                        eng = "vector"
                        b.wait(eng, tmk)
                        dst = qtile(widx, hh)[0:64, :]
                        src = PW[widx][64 * hh:64 * hh + 64, 0:1024]
                        if eng == "vector":
                            b.op("vector", lambda e, d=dst, s=src:
                                 e.tensor_copy(d, s), inc="V")
                        else:
                            b.op("scalar", lambda e, d=dst, s=src:
                                 e.copy(d, s), inc="S")
                qevac = [b.mark("V"), b.mark("S")]
                if _DUMP == "proj0" and hp == 0 and qh == 0:
                    b.wait("gpsimd", *qevac, *kevac, *vevac,
                           kaug_mark, qaug_mark)
                    finish_dump(b.mark("P"), WARE[:, 0:16384], True)
                    return nc

                for hh in range(2):
                    DK = DTAB[2 * hp + hh]
                    ktmax = 8 if qh == 0 else 16
                    kt_lo = max(0, qh * 8 - DK + 1)
                    kt7 = max(0, qh * 8 + 8 - DK)
                    exp_done = {}
                    b.wait("tensor", kaug_mark, qaug_mark,
                           *qevac, *kevac, *vevac)

                    def emit_pv(j, kt, DK=DK, kt_lo=kt_lo, kt7=kt7,
                                ktmax=ktmax, hh=hh, qh=qh):
                        qc0 = max(0, kt * 128 - qh * 1024)
                        qt_hi = min(8, kt + DK - qh * 8)
                        b.wait("tensor", exp_done[(j, kt)])
                        if kt == kt_lo and j == 0:
                            b.wait("tensor", *po_free)
                        for qt in range(qc0 // 128, qt_hi):
                            dest = (PO[j][:, qt * 65:qt * 65 + 65]
                                    if qt < 7
                                    else PR1[:, j * 65:j * 65 + 65])
                            if qt < 7:
                                mm_start = (kt == kt_lo and qt == 0)
                                mm_stop = (kt == ktmax - 2 and qt == 6)
                            else:
                                mm_start = (kt == kt7 and j == 0)
                                mm_stop = (kt == ktmax - 1 and j == 1)
                            b.op("tensor", lambda e, qt=qt, kt=kt,
                                 hh=hh, dest=dest, j=j,
                                 mm_start=mm_start, mm_stop=mm_stop:
                                 e.matmul(
                                     dest,
                                     PTJ[j][kt % 2][:, qt * 128:qt * 128 + 128],
                                     VTM[:, kt * 130 + hh * 65:
                                         kt * 130 + hh * 65 + 65],
                                     start=mm_start, stop=mm_stop),
                                 inc="T")

                    for kt in range(kt_lo, ktmax):
                        qc0 = max(0, kt * 128 - qh * 1024)
                        qc_hi = min(1024, (kt + DK) * 128 - qh * 1024)
                        has_mask = kt * 128 >= qh * 1024
                        pieces = []
                        if qc0 < 512:
                            pieces.append((qc0, min(512, qc_hi),
                                           True, not has_mask))
                            if qc_hi > 512:
                                pieces.append((512, qc_hi, True, True))
                        else:
                            pieces.append((qc0, qc_hi, True, not has_mask))
                        for j in range(2):
                            b.wait("tensor", exp_free[j])
                            for (a0, a1, st, sp) in pieces:
                                b.op("tensor", lambda e, j=j, hh=hh, kt=kt,
                                     a0=a0, a1=a1, st=st, sp=sp:
                                     e.matmul(
                                         PW[j][:, a0:a1],
                                         ktile(j, hh)[:,
                                                      kt * 128:kt * 128 + 128],
                                         qtile(j, hh)[:, a0:a1],
                                         start=st, stop=sp), inc="T")
                            if has_mask:
                                b.op("tensor", lambda e, j=j, qc0=qc0:
                                     e.matmul(PW[j][:, qc0:qc0 + 128],
                                              IDEN[:], MASKT[:],
                                              start=False, stop=True), inc="T")
                            st_mark = b.mark("T")
                            b.wait("scalar", st_mark, pt_free[(j, kt % 2)])
                            b.op("scalar", lambda e, j=j, qc0=qc0, kt=kt,
                                 qc_hi=qc_hi: e.activation(
                                     PTJ[j][kt % 2][:, qc0:qc_hi],
                                     PW[j][:, qc0:qc_hi], AF.Exp,
                                     scale=SCALE), inc="S")
                            exp_done[(j, kt)] = b.mark("S")
                            exp_free[j] = exp_done[(j, kt)]
                        if kt > kt_lo:
                            emit_pv(0, kt - 1)
                            pt_free[(0, (kt - 1) % 2)] = b.mark("T")
                            emit_pv(1, kt - 1)
                            pt_free[(1, (kt - 1) % 2)] = b.mark("T")
                    emit_pv(0, ktmax - 1)
                    pt_free[(0, (ktmax - 1) % 2)] = b.mark("T")
                    emit_pv(1, ktmax - 1)
                    pt_free[(1, (ktmax - 1) % 2)] = b.mark("T")
                    pv_done = b.mark("T")

                    h = 2 * hp + hh
                    b.wait("vector", pv_done)

                    def ve(fn):
                        b.op("vector", fn, inc="V")
                        b.wait("vector", b.mark("V"))

                    def r_b(c0, n):
                        return bass.AP(RECIPS, c0, [[32, 128], [1, n], [0, 64]])

                    def mu_b(c0, n):
                        return bass.AP(MUVAR, c0, [[32, 128], [1, n], [0, 64]])
                    CSB_ = TMP[:, 0:512]
                    T2SB_ = TMP[:, 512:1024]
                    csb_v = bass.AP(TMP, 0, [[2048, 128], [64, 8], [1, 64]])
                    t2_v = bass.AP(TMP, 512, [[2048, 128], [64, 8], [1, 64]])
                    ve(lambda e: e.tensor_copy(ROWS[:, 0:455], PO1[:, 0:455]))
                    ve(lambda e: e.tensor_copy(ROWS[:, 520:975], PO2[:, 0:455]))
                    ve(lambda e: e.tensor_copy(
                        bass.AP(ROWS, 455, [[2048, 128], [520, 2], [1, 65]]),
                        PR1[:, 0:130].rearrange("p (j c) -> p j c", j=2)))
                    po_free = [b.mark("V")]
                    ve(lambda e: e.reciprocal(
                        RECIPS[:, 0:7], bass.AP(ROWS, 64, [[2048, 128], [65, 7]])))
                    ve(lambda e: e.reciprocal(RECIPS[:, 7:8], ROWS[:, 519:520]))
                    ve(lambda e: e.reciprocal(
                        RECIPS[:, 8:15],
                        bass.AP(ROWS, 584, [[2048, 128], [65, 7]])))
                    ve(lambda e: e.reciprocal(RECIPS[:, 15:16],
                                              ROWS[:, 1039:1040]))
                    ve(lambda e: e.tensor_scalar_mul(
                        RECIPS[:, 16:24], RECIPS[:, 8:16], NLAM[:, 0:1]))
                    ve(lambda e: e.tensor_tensor(
                        csb_v[:, 0:7, :],
                        bass.AP(ROWS, 0, [[2048, 128], [65, 7], [1, 64]]),
                        r_b(0, 7), ALU.mult))
                    ve(lambda e: e.tensor_tensor(
                        csb_v[:, 7:8, :],
                        bass.AP(ROWS, 455, [[2048, 128], [65, 1], [1, 64]]),
                        r_b(7, 1), ALU.mult))
                    ve(lambda e: e.tensor_tensor(
                        t2_v[:, 0:7, :],
                        bass.AP(ROWS, 520, [[2048, 128], [65, 7], [1, 64]]),
                        r_b(16, 7), ALU.mult))
                    ve(lambda e: e.tensor_tensor(
                        t2_v[:, 7:8, :],
                        bass.AP(ROWS, 975, [[2048, 128], [65, 1], [1, 64]]),
                        r_b(23, 1), ALU.mult))
                    ve(lambda e: e.tensor_add(CSB_, CSB_, T2SB_))
                    ve(lambda e: e.reduce_sum(
                        MUVAR[:, 0:8].rearrange("p (t o) -> p t o", t=8),
                        csb_v, axis=AX.X))
                    ve(lambda e: e.tensor_scalar_mul(
                        MUVAR[:, 0:8], MUVAR[:, 0:8], 1.0 / DH))
                    ve(lambda e: e.tensor_tensor(
                        t2_v, csb_v, mu_b(0, 8), ALU.subtract))
                    ve(lambda e: e.tensor_mul(CSB_, T2SB_, T2SB_))
                    ve(lambda e: e.reduce_sum(
                        MUVAR[:, 8:16].rearrange("p (t o) -> p t o", t=8),
                        csb_v, axis=AX.X))
                    vgn = b.mark("V")
                    b.wait("scalar", vgn, ("P", 2))
                    b.op("scalar", lambda e: e.activation(
                        MUVAR[:, 16:24], MUVAR[:, 8:16], AF.Ln,
                        bias=EPSC[:], scale=1.0 / DH), inc="S")
                    b.wait("scalar", b.mark("S"))
                    b.op("scalar", lambda e: e.activation(
                        MUVAR[:, 8:16], MUVAR[:, 16:24], AF.Exp, scale=-0.5),
                        inc="S")
                    sgn = b.mark("S")
                    b.wait("vector", sgn)
                    b.op("vector", lambda e, h=h, qh=qh: e.tensor_tensor(
                        bass.AP(A2, (qh * 8) * 1024 + h * 64,
                                [[16384, 128], [1024, 8], [1, 64]]),
                        t2_v, mu_b(8, 8), ALU.mult), inc="V")
                qk_war = [b.mark("T")]
            vtm_war = [b.mark("T")]
            slice_war[hp % 2] = [b.mark("T")]
        attn_core_V = b.mark("V")
        attn_core_T = b.mark("T")
        if _DUMP == "attncore":
            finish_dump(attn_core_V, A2[:], True)
            return nc

        b.wait("gpsimd", *slice_war[0], *slice_war[1])
        b.op("gpsimd", lambda e: e.dma_start(
            out=WARE[:, WO_OFF:WO_OFF + 8192].rearrange(
                "p (eb d) -> p eb d", eb=CB),
            in_=wo_d.rearrange("(eb p) d -> p eb d", p=128)), inc="DW", n=16)
        dg_wo = b.mark("DW")
        b.wait("tensor", dg_wo, dg_lnb, attn_core_T, *po_free)
        for db in range(CB):
            for eb in range(CB):
                b.op("tensor", lambda e, db=db, eb=eb: e.matmul(
                    PR1[:, db:db + 1],
                    WARE[:, WO_OFF + eb * 1024 + db * 128:
                         WO_OFF + eb * 1024 + db * 128 + 128],
                    LNBC[:, eb:eb + 1],
                    start=(eb == 0 and db == 0),
                    stop=(eb == 7 and db == 7)), inc="T")
        tbias = b.mark("T")
        b.wait("vector", tbias)
        b.op("vector", lambda e: e.tensor_copy(BIASD[:], PR1[:, 0:8]), inc="V")
        vbias = b.mark("V")
        b.wait("vector", tbias)
        for eb in range(CB):
            b.op("vector", lambda e, eb=eb: e.tensor_scalar_mul(
                WARE[:, WO_OFF + eb * 1024:WO_OFF + eb * 1024 + 1024],
                WARE[:, WO_OFF + eb * 1024:WO_OFF + eb * 1024 + 1024],
                LNWC[:, eb:eb + 1]), inc="V")
        p_wo = b.mark("V")
        b.wait("tensor", attn_core_V)
        tr_evac = {}
        for r in range(32):
            if r >= 1:
                b.wait("tensor", tr_evac[r - 1])
            for s in range(4):
                idx = r * 4 + s
                tt, eb = idx // 8, idx % 8
                b.op("tensor", lambda e, tt=tt, eb=eb, s=s: e.transpose(
                    PTR[:, s * 128:s * 128 + 128],
                    A2[:, tt * 1024 + eb * 128: tt * 1024 + eb * 128 + 128],
                    IDEN[:]), inc="T")
            tmk = b.mark("T")
            b.wait("vector", tmk)
            for s in range(4):
                idx = r * 4 + s
                tt, eb = idx // 8, idx % 8
                b.op("vector", lambda e, tt=tt, eb=eb, s=s: e.tensor_copy(
                    XLN[:, eb * 2048 + tt * 128: eb * 2048 + tt * 128 + 128],
                    PTR[:, s * 128:s * 128 + 128]), inc="V")
            tr_evac[r] = b.mark("V")
        tr_done = b.mark("V")
        b.wait("tensor", tr_done, p_wo)
        b.wait("vector", vbias)
        wo_evac = {}
        for db in range(CB):
            for half in range(2):
                k = db * 2 + half
                par = k % 2
                if k >= 2:
                    b.wait("tensor", wo_evac[k - 2])
                for eb in range(CB):
                    for nch in range(2):
                        b.op("tensor", lambda e, db=db, eb=eb, half=half,
                             nch=nch, par=par: e.matmul(
                                 PW[par][:, nch * 512:nch * 512 + 512],
                                 WARE[:, WO_OFF + eb * 1024 + db * 128:
                                      WO_OFF + eb * 1024 + db * 128 + 128],
                                 XLN[:, eb * 2048 + half * 1024 + nch * 512:
                                     eb * 2048 + half * 1024 + nch * 512 + 512],
                                 start=(eb == 0), stop=(eb == 7)), inc="T")
                tmk = b.mark("T")
                b.wait("vector", tmk)
                b.op("vector", lambda e, db=db, half=half, par=par:
                     e.scalar_tensor_tensor(
                         R[:, db * 2048 + half * 1024:
                           db * 2048 + half * 1024 + 1024],
                         R[:, db * 2048 + half * 1024:
                           db * 2048 + half * 1024 + 1024],
                         1.0, PW[par][:, 0:1024], ALU.mult, ALU.add), inc="V")
                b.wait("vector", b.mark("V"))
                b.op("vector", lambda e, db=db, half=half: e.tensor_scalar_add(
                    R[:, db * 2048 + half * 1024:
                      db * 2048 + half * 1024 + 1024],
                    R[:, db * 2048 + half * 1024:
                      db * 2048 + half * 1024 + 1024],
                    BIASD[:, db:db + 1]), inc="V")
                wo_evac[k] = b.mark("V")
        pw_free[0] = [wo_evac[14]]
        pw_free[1] = [wo_evac[15]]
        attn_done = b.mark("V")

        if _DUMP == "attn":
            finish_dump(attn_done, R[:], False)
            return nc
        b.wait("gpsimd", attn_done)
        for off, src in ((WK_OFF, wkt_d), (WV_OFF, wvt_d), (WR_OFF, wrt_d)):
            b.op("gpsimd", lambda e, o=off, s=src: e.dma_start(
                out=WARE[:, o:o + 8192].rearrange("p (cb o) -> p cb o", cb=CB),
                in_=s.rearrange("(cb p) o -> p cb o", p=128)), inc="DW", n=16)
        dg_ffnw2 = b.mark("DW")
        cast3 = emit_cast(attn_done, [b.mark("T")])
        xln3 = emit_ln(cast3, [attn_done])
        if _DUMP == "xln3":
            finish_dump(xln3, XLN[:], True)
            return nc
        ffn2 = emit_ffn(dg_ffnw2, xln3, tr_done)

        b.wait("sync", ffn2)
        b.op("sync", lambda e: e.dma_start(
            out=y_d.rearrange("(cb p) t -> p cb t", p=128),
            in_=R[:].rearrange("p (cb t) -> p cb t", cb=CB)), inc="DX", n=16)
        b.wait("sync", "DX")

        with nc.Block() as block:
            b.emit(block)
    return nc


def _host_inputs(ins):
    f32 = np.float32
    sh = {}
    for name, key in (("wq0", "L0_attn_wq0"), ("wq1", "L0_attn_wq1"),
                      ("wk0", "L0_attn_wk0"), ("wk1", "L0_attn_wk1"),
                      ("wv", "L0_attn_wv")):
        w = np.asarray(ins[key], f32)
        sh[name] = np.ascontiguousarray(
            w.reshape(D, 8, 128).transpose(1, 0, 2))
    sh["wo"] = np.ascontiguousarray(np.asarray(ins["L0_attn_wo"], f32))
    sh["wrt"] = np.ascontiguousarray(np.asarray(ins["L0_ffn_W_r"], f32).T)
    sh["wkt"] = np.ascontiguousarray(np.asarray(ins["L0_ffn_W_k"], f32).T)
    sh["wvt"] = np.ascontiguousarray(np.asarray(ins["L0_ffn_W_v"], f32).T)
    sh["mur"] = np.ascontiguousarray(
        np.asarray(ins["L0_ffn_mu_r"], f32).reshape(8, 128).T)
    sh["muk"] = np.ascontiguousarray(
        np.asarray(ins["L0_ffn_mu_k"], f32).reshape(8, 128).T)
    sh["lnw"] = np.ascontiguousarray(
        np.asarray(ins["L0_attn_ln_w"], f32).reshape(-1).reshape(8, 128).T)
    sh["lnb"] = np.ascontiguousarray(
        np.asarray(ins["L0_attn_ln_b"], f32).reshape(-1).reshape(8, 128).T)
    lv = np.stack([np.asarray(ins["L0_attn_l0"], f32),
                   np.asarray(ins["L0_attn_l1"], f32),
                   np.asarray(ins["L0_attn_l2"], f32),
                   np.asarray(ins["L0_attn_l3"], f32)], axis=1)
    sh["lvec"] = np.ascontiguousarray(lv)

    kk, qq = np.meshgrid(np.arange(128), np.arange(128), indexing="ij")
    sh["maskt"] = np.where(kk > qq, -1e30, 0.0).astype(BF)
    sh["iden"] = np.eye(128).astype(BF)
    sh["onesc"] = np.ones((128, 1), BF)
    sh["onesf"] = np.ones((65, 128), f32)
    slopes = _slopes()
    t_idx = np.arange(T, dtype=np.float64)
    kaug = np.zeros((8, 7, 8192), np.float64)
    for hp in range(8):
        for blk in range(4):
            h = 2 * hp + blk // 2
            ks = _bf16_splits(8.0 * slopes[h] * t_idx, NKR)
            kaug[hp, 0:NKR, blk * 2048:(blk + 1) * 2048] = ks
            kaug[hp, NKR:7, blk * 2048:(blk + 1) * 2048] = 1.0
    sh["kaug"] = kaug.astype(BF)
    qaug = np.zeros((8, 2, 7, 4096), np.float64)
    for hp in range(8):
        for qh in range(2):
            for blk in range(4):
                h = 2 * hp + blk // 2
                qs = _bf16_splits(
                    -8.0 * slopes[h] * (qh * 1024 + np.arange(1024)), NQR)
                qaug[hp, qh, 0:NKR, blk * 1024:(blk + 1) * 1024] = 1.0
                qaug[hp, qh, NKR:7, blk * 1024:(blk + 1) * 1024] = qs
    sh["qaug"] = qaug.astype(BF)
    return sh


def kernel(**inputs):
    ins = {k: np.asarray(v) for k, v in inputs.items()}
    sh = _host_inputs(ins)
    x = np.asarray(ins["L0_attn_inputs"], np.float32)
    nc = build_nc()
    in_maps = []
    for bidx in range(8):
        m = dict(sh)
        m["x"] = np.ascontiguousarray(x[bidx].T)
        in_maps.append(m)
    res = run_bass_kernel_spmd(nc, in_maps, core_ids=list(range(8)))
    return np.stack([np.asarray(res.results[i]["y"]).T for i in range(8)])


if __name__ == "__main__":
    import reference as ref
    ins = ref.setup_inputs()
    out = kernel(**{k: np.asarray(v) for k, v in ins.items()})
    expect = np.asarray(ref.reference(**ins))
    err = np.abs(out - expect)
    rel = np.sqrt(((out - expect) ** 2).mean() / (expect ** 2).mean())
    print("rel_l2", rel, "max_abs", err.max())


# version 19
# speedup vs baseline: 1.0308x; 1.0308x over previous
import numpy as np
import ml_dtypes
from contextlib import ExitStack
import concourse.bass as bass
import concourse.mybir as mybir
from concourse.bass_utils import run_bass_kernel_spmd

F32 = mybir.dt.float32
BF16 = mybir.dt.bfloat16
AF = mybir.ActivationFunctionType
ALU = mybir.AluOpType
AX = mybir.AxisListType
BF = ml_dtypes.bfloat16

T = 2048
D = 1024
H = 16
DH = 64
CB = 8
NKR, NQR = 5, 2
NAUG = NKR + NQR
KD = DH + NAUG
SCALE = 0.125
EPS = 1e-5
LAMBDA_INIT = 0.8


def _bf16_splits(v, n):
    parts, r = [], np.asarray(v, np.float64).copy()
    for _ in range(n):
        p = r.astype(BF).astype(np.float64)
        parts.append(p)
        r = r - p
    return np.stack(parts)


def _slopes():
    return 2.0 ** (-8.0 * (np.arange(1, H + 1, dtype=np.float64)) / H)


TAU = 30.0
DTAB = [min(16, int((np.ceil(TAU / s) + 127) // 128) + 1) for s in _slopes()]


class Bld:
    ENGS = ("sync", "gpsimd", "tensor", "scalar", "vector")

    def __init__(self, nc, stack):
        self.nc = nc
        self.prog = {e: [] for e in self.ENGS}
        self.ev = {}
        for name in ("T", "S", "V", "P", "DX", "DC", "DXB", "DW",
                     "DS0", "DS1", "DKA", "DQA", "DD", "DL"):
            sem = stack.enter_context(nc.semaphore("ev" + name))
            self.ev[name] = [sem, 0]

    def wait(self, eng, *marks):
        seen = {}
        for m in marks:
            if isinstance(m, str):
                name, k = m, self.ev[m][1]
            else:
                name, k = m
            if k > seen.get(name, 0):
                seen[name] = k
        for name, k in seen.items():
            sem = self.ev[name][0]
            self.prog[eng].append(lambda e, s=sem, kk=k: e.wait_ge(s, kk))

    def op(self, eng, fn, inc=None, n=1):
        if inc is None:
            self.prog[eng].append(fn)
        else:
            rec = self.ev[inc]
            rec[1] += n
            self.prog[eng].append(
                lambda e, fn=fn, s=rec[0], nn=n: fn(e).then_inc(s, nn))

    def mark(self, name):
        return (name, self.ev[name][1])

    def emit(self, block):
        for eng in self.ENGS:
            ops = self.prog[eng]
            if ops:
                getattr(block, eng)(
                    lambda engine, ops=ops: [f(engine) for f in ops])


_DUMP = None


def build_nc():
    nc = bass.Bass("TRN2", target_bir_lowering=False)
    dt = nc.dram_tensor
    x_d = dt("x", [D, T], F32, kind="ExternalInput").ap()
    y_d = dt("y", [D, T], F32, kind="ExternalOutput").ap()
    wq0_d = dt("wq0", [8, D, 128], F32, kind="ExternalInput").ap()
    wq1_d = dt("wq1", [8, D, 128], F32, kind="ExternalInput").ap()
    wk0_d = dt("wk0", [8, D, 128], F32, kind="ExternalInput").ap()
    wk1_d = dt("wk1", [8, D, 128], F32, kind="ExternalInput").ap()
    wv_d = dt("wv", [8, D, 128], F32, kind="ExternalInput").ap()
    wo_d = dt("wo", [D, D], F32, kind="ExternalInput").ap()
    wrt_d = dt("wrt", [D, D], F32, kind="ExternalInput").ap()
    wkt_d = dt("wkt", [D, D], F32, kind="ExternalInput").ap()
    wvt_d = dt("wvt", [D, D], F32, kind="ExternalInput").ap()
    mur_d = dt("mur", [128, 8], F32, kind="ExternalInput").ap()
    muk_d = dt("muk", [128, 8], F32, kind="ExternalInput").ap()
    lnw_d = dt("lnw", [128, 8], F32, kind="ExternalInput").ap()
    lnb_d = dt("lnb", [128, 8], F32, kind="ExternalInput").ap()
    lv_d = dt("lvec", [64, 4], F32, kind="ExternalInput").ap()
    iden_d = dt("iden", [128, 128], BF16, kind="ExternalInput").ap()
    mask_d = dt("maskt", [128, 128], BF16, kind="ExternalInput").ap()
    onesc_d = dt("onesc", [128, 1], BF16, kind="ExternalInput").ap()
    onesf_d = dt("onesf", [65, 128], F32, kind="ExternalInput").ap()
    kaug_d = dt("kaug", [8, 7, 8192], BF16, kind="ExternalInput").ap()
    qaug_d = dt("qaug", [8, 2, 7, 4096], BF16, kind="ExternalInput").ap()

    with ExitStack() as st:
        e = st.enter_context
        R = e(nc.sbuf_tensor("R", [128, 16384], F32))
        XLN = e(nc.sbuf_tensor("XLN", [128, 16384], BF16))
        A2 = e(nc.sbuf_tensor("A2", [128, 16384], BF16))
        WARE = e(nc.sbuf_tensor("WARE", [128, 24576], BF16))
        VTM = e(nc.sbuf_tensor("VTM", [128, 2080], BF16))
        PT00 = e(nc.sbuf_tensor("PT00", [128, 1024], BF16))
        PT01 = e(nc.sbuf_tensor("PT01", [128, 1024], BF16))
        PT10 = e(nc.sbuf_tensor("PT10", [128, 1024], BF16))
        PT11 = e(nc.sbuf_tensor("PT11", [128, 1024], BF16))
        ROWS = e(nc.sbuf_tensor("ROWS", [128, 2048], F32))
        TMP = e(nc.sbuf_tensor("TMP", [128, 2048], F32))
        SQS = e(nc.sbuf_tensor("SQS", [128, 1024], BF16))
        RECIPS = e(nc.sbuf_tensor("RECIPS", [128, 32], F32))
        MUVAR = e(nc.sbuf_tensor("MUVAR", [128, 32], F32))
        BND = e(nc.sbuf_tensor("BND", [128, 8], BF16))
        IDEN = e(nc.sbuf_tensor("IDEN", [128, 128], BF16))
        MASKT = e(nc.sbuf_tensor("MASKT", [128, 128], BF16))
        ONESC = e(nc.sbuf_tensor("ONESC", [128, 1], BF16))
        ONESF = e(nc.sbuf_tensor("ONESF", [65, 128], F32))
        MURC = e(nc.sbuf_tensor("MURC", [128, 8], F32))
        MUKC = e(nc.sbuf_tensor("MUKC", [128, 8], F32))
        MURB = e(nc.sbuf_tensor("MURB", [128, 8], F32))
        MUKB = e(nc.sbuf_tensor("MUKB", [128, 8], F32))
        LNWC = e(nc.sbuf_tensor("LNWC", [128, 8], F32))
        LNBC = e(nc.sbuf_tensor("LNBC", [128, 8], BF16))
        LVEC = e(nc.sbuf_tensor("LVEC", [64, 4], F32))
        NLAM = e(nc.sbuf_tensor("NLAM", [128, 1], F32))
        BIASD = e(nc.sbuf_tensor("BIASD", [128, 8], F32))
        SC1 = e(nc.sbuf_tensor("SC1", [1, 8], F32))
        EPSC = e(nc.sbuf_tensor("EPSC", [128, 1], F32))
        PW1 = e(nc.psum_tensor("PW1", [128, 1024], F32))
        PW2 = e(nc.psum_tensor("PW2", [128, 1024], F32))
        PO1 = e(nc.psum_tensor("PO1", [128, 512], F32))
        PO2 = e(nc.psum_tensor("PO2", [128, 512], F32))
        PR1 = e(nc.psum_tensor("PR1", [128, 512], F32))
        PTR = e(nc.psum_tensor("PTR", [128, 512], BF16))
        b = Bld(nc, st)
        PW = (PW1, PW2)
        PO = (PO1, PO2)

        pw_free = [[], []]
        po_free = []

        WK_OFF, WV_OFF, WR_OFF = 0, 8192, 16384

        def qtile(j, hh):
            return WARE[0:KD, (j + 2 * hh) * 1024:(j + 2 * hh) * 1024 + 1024]

        def ktile(j, hh):
            off = 4096 + (j + 2 * hh) * 2048
            return WARE[0:KD, off:off + 2048]

        SLC = (12288, 17408)
        WO_OFF = 12288

        def slice_view(hp, widx, cbk):
            base = SLC[hp % 2] + widx * 1024
            return WARE[:, base + cbk * 128: base + cbk * 128 + 128]

        x_r = x_d.rearrange("(cb p) t -> p cb t", p=128)
        b.op("sync", lambda e: e.dma_start(
            out=R[:].rearrange("p (cb t) -> p cb t", cb=CB), in_=x_r),
            inc="DX", n=16)
        ds_x = b.mark("DX")
        for (dst, src) in ((IDEN, iden_d), (MASKT, mask_d), (ONESC, onesc_d),
                           (ONESF, onesf_d), (MURC, mur_d), (MUKC, muk_d),
                           (LNWC, lnw_d), (LVEC, lv_d)):
            b.op("sync", lambda e, d=dst, s=src: e.dma_start(out=d[:], in_=s),
                 inc="DC", n=16)
        ds_consts = b.mark("DC")
        b.op("gpsimd", lambda e: e.dma_start(out=LNBC[:], in_=lnb_d),
             inc="DL", n=16)
        dg_lnb = b.mark("DL")
        b.op("gpsimd", lambda e: e.dma_start(
            out=XLN[:].rearrange("p (cb t) -> p cb t", cb=CB), in_=x_r),
            inc="DXB", n=16)
        dg_xbf = b.mark("DXB")
        for off, src in ((WK_OFF, wkt_d), (WV_OFF, wvt_d), (WR_OFF, wrt_d)):
            b.op("gpsimd", lambda e, o=off, s=src: e.dma_start(
                out=WARE[:, o:o + 8192].rearrange("p (cb o) -> p cb o", cb=CB),
                in_=s.rearrange("(cb p) o -> p cb o", p=128)), inc="DW", n=16)
        dg_ffnw1 = b.mark("DW")
        b.wait("gpsimd", ds_consts)
        b.op("gpsimd", lambda e: e.memset(
            bass.AP(VTM, 64, [[2080, 128], [65, 32]]), 1.0), inc="P")
        b.op("gpsimd", lambda e: e.memset(EPSC[:], EPS), inc="P")
        p_vones = b.mark("P")
        b.wait("vector", ds_consts)
        b.op("vector", lambda e: e.tensor_scalar(
            MURB[:], MURC[:], -1.0, 1.0, ALU.mult, ALU.add), inc="V")
        b.op("vector", lambda e: e.tensor_scalar(
            MUKB[:], MUKC[:], -1.0, 1.0, ALU.mult, ALU.add), inc="V")
        v_params = b.mark("V")

        def emit_ln(xln_ready, r_marks):
            sq_free = [b.mark("T"), b.mark("T")]
            b.wait("vector", *xln_ready)
            b.wait("tensor", *xln_ready, ds_consts,
                   *pw_free[0], *pw_free[1], *po_free)
            for g in range(4):
                for cb in range(CB):
                    par = cb % 2
                    xv = XLN[:, cb * 2048 + g * 512: cb * 2048 + g * 512 + 512]
                    b.wait("vector", sq_free[par])
                    b.op("vector", lambda e, xv=xv, par=par: e.tensor_mul(
                        SQS[:, par * 512:par * 512 + 512], xv, xv), inc="V")
                    vmk = b.mark("V")
                    b.wait("tensor", vmk)
                    b.op("tensor", lambda e, xv=xv, cb=cb: e.matmul(
                        PR1[0:1, 0:512], ONESC[:], xv,
                        start=(cb == 0), stop=(cb == 7)), inc="T")
                    b.op("tensor", lambda e, par=par, cb=cb: e.matmul(
                        PO1[0:1, 0:512], ONESC[:],
                        SQS[:, par * 512:par * 512 + 512],
                        start=(cb == 0), stop=(cb == 7)), inc="T")
                    sq_free[par] = b.mark("T")
                tmk = b.mark("T")
                b.wait("vector", tmk)
                b.op("vector", lambda e, g=g: e.tensor_copy(
                    ROWS[0:1, g * 512:g * 512 + 512], PR1[0:1, 0:512]), inc="V")
                b.op("vector", lambda e, g=g: e.tensor_copy(
                    ROWS[32:33, g * 512:g * 512 + 512], PO1[0:1, 0:512]), inc="V")
                b.wait("tensor", b.mark("V"))
            def sev(fn):
                b.op("vector", fn, inc="V")
                b.wait("vector", b.mark("V"))

            def ses(fn):
                b.op("scalar", fn, inc="S")
                b.wait("scalar", b.mark("S"))
            b.wait("scalar", b.mark("V"), ("P", 2))
            b.wait("vector", b.mark("V"))
            ses(lambda e: e.activation(
                TMP[32:33, :], ROWS[0:1, :], AF.Square, scale=1.0 / D))
            smsq = b.mark("S")
            b.wait("vector", smsq)
            sev(lambda e: e.tensor_scalar_mul(ROWS[0:1, :], ROWS[0:1, :], 1.0 / D))
            sev(lambda e: e.scalar_tensor_tensor(
                ROWS[32:33, :], ROWS[32:33, :], 1.0 / D, TMP[32:33, :],
                ALU.mult, ALU.subtract))
            vrow = b.mark("V")
            b.wait("scalar", vrow)
            ses(lambda e: e.activation(
                ROWS[64:65, :], ROWS[32:33, :], AF.Ln, bias=EPSC[0:1, :],
                scale=1.0))
            ses(lambda e: e.activation(
                ROWS[32:33, :], ROWS[64:65, :], AF.Exp, scale=-0.5))
            srow = b.mark("S")
            b.wait("tensor", srow, vrow)
            b.wait("vector", *r_marks)
            for g in range(4):
                b.op("tensor", lambda e, g=g: e.matmul(
                    PO1[:, 0:512], ONESF[0:1, :],
                    ROWS[0:1, g * 512:g * 512 + 512], start=True, stop=True),
                    inc="T")
                b.op("tensor", lambda e, g=g: e.matmul(
                    PO2[:, 0:512], ONESF[32:33, :],
                    ROWS[32:33, g * 512:g * 512 + 512], start=True, stop=True),
                    inc="T")
                tmk = b.mark("T")
                b.wait("vector", tmk)
                for cb in range(CB):
                    par = cb % 2
                    off = cb * 2048 + g * 512
                    b.op("vector", lambda e, off=off, par=par: e.tensor_sub(
                        TMP[:, par * 512:par * 512 + 512],
                        R[:, off:off + 512], PO1[:, 0:512]), inc="V")
                    b.wait("vector", b.mark("V"))
                    b.op("vector", lambda e, off=off, par=par: e.tensor_mul(
                        XLN[:, off:off + 512],
                        TMP[:, par * 512:par * 512 + 512], PO2[:, 0:512]),
                        inc="V")
                vmk = b.mark("V")
                b.wait("tensor", vmk)
            return b.mark("V")

        def emit_ffn(dg_w_mark, xln_mark, a2_free):
            nonlocal pw_free
            last_v = xln_mark
            for qh in range(2):
                b.wait("vector", xln_mark, a2_free, v_params, "S")
                stt_done = [None, None]
                mi = 0
                for (ydst, ac, bcol) in ((0, MUKC, MUKB), (8192, MURC, MURB)):
                    for cb in range(CB):
                        sl = mi % 2
                        mi += 1
                        to = sl * 1024
                        xc = XLN[:, cb * 2048 + qh * 1024:
                                 cb * 2048 + qh * 1024 + 1024]
                        yv = A2[:, ydst + cb * 1024: ydst + cb * 1024 + 1024]
                        if stt_done[sl] is not None:
                            b.wait("vector", stt_done[sl])
                        b.op("vector", lambda e, xc=xc, bcol=bcol, cb=cb, to=to:
                             e.tensor_scalar_mul(TMP[:, to:to + 1023],
                                                 xc[:, 0:1023],
                                                 bcol[:, cb:cb + 1]), inc="V")
                        if qh == 1:
                            b.op("vector", lambda e, bcol=bcol, cb=cb, to=to:
                                 e.tensor_scalar_mul(TMP[:, to + 1023:to + 1024],
                                                     BND[:, cb:cb + 1],
                                                     bcol[:, cb:cb + 1]), inc="V")
                        b.wait("vector", b.mark("V"))
                        b.op("vector", lambda e, xc=xc, ac=ac, cb=cb, yv=yv,
                             to=to: e.scalar_tensor_tensor(
                                 yv[:, 1:1024], xc[:, 1:1024], ac[:, cb:cb + 1],
                                 TMP[:, to:to + 1023], ALU.mult, ALU.add),
                             inc="V")
                        if qh == 0:
                            b.op("vector", lambda e, xc=xc, ac=ac, cb=cb, yv=yv:
                                 e.tensor_scalar_mul(yv[:, 0:1], xc[:, 0:1],
                                                     ac[:, cb:cb + 1]), inc="V")
                        else:
                            b.op("vector", lambda e, xc=xc, ac=ac, cb=cb, yv=yv,
                                 to=to: e.scalar_tensor_tensor(
                                     yv[:, 0:1], xc[:, 0:1], ac[:, cb:cb + 1],
                                     TMP[:, to + 1023:to + 1024],
                                     ALU.mult, ALU.add), inc="V")
                        stt_done[sl] = b.mark("V")
                if qh == 0:
                    b.op("vector", lambda e: e.tensor_copy(
                        BND[:], bass.AP(XLN, 1023, [[16384, 128], [2048, 8]])),
                        inc="V")
                    b.wait("vector", b.mark("V"))
                ymk = b.mark("V")

                b.wait("tensor", ymk, dg_w_mark, *pw_free[0], *pw_free[1])
                relu_mark, sq_mark = {}, {}
                for ob in range(CB):
                    par = ob % 2
                    if ob >= 2:
                        b.wait("tensor", relu_mark[ob - 2])
                    for cb in range(CB):
                        for nch in range(2):
                            b.op("tensor", lambda e, ob=ob, cb=cb, nch=nch,
                                 par=par, qh=qh: e.matmul(
                                     PW[par][:, nch * 512:nch * 512 + 512],
                                     WARE[:, WK_OFF + cb * 1024 + ob * 128:
                                          WK_OFF + cb * 1024 + ob * 128 + 128],
                                     A2[:, cb * 1024 + nch * 512:
                                        cb * 1024 + nch * 512 + 512],
                                     start=(cb == 0), stop=(cb == 7)), inc="T")
                    tmk = b.mark("T")
                    b.wait("scalar", tmk)
                    if ob >= 2:
                        b.wait("scalar", sq_mark[ob - 2])
                    b.op("scalar", lambda e, par=par: e.activation(
                        TMP[:, par * 1024:par * 1024 + 1024],
                        PW[par][:, 0:1024], AF.Relu), inc="S")
                    relu_mark[ob] = b.mark("S")
                    b.wait("vector", relu_mark[ob])
                    b.op("vector", lambda e, ob=ob, par=par, qh=qh: e.tensor_mul(
                        XLN[:, ob * 2048 + qh * 1024:
                            ob * 2048 + qh * 1024 + 1024],
                        TMP[:, par * 1024:par * 1024 + 1024],
                        TMP[:, par * 1024:par * 1024 + 1024]), inc="V")
                    sq_mark[ob] = b.mark("V")
                kact_mark = b.mark("V")

                b.wait("tensor", kact_mark, relu_mark[6], relu_mark[7])
                prod_mark, sig_mark = {}, {}
                for u in range(2 * CB):
                    ob, nch = u // 2, u % 2
                    c0 = qh * 1024 + nch * 512
                    if u >= 2:
                        b.wait("tensor", prod_mark[u - 2])
                    for cb in range(CB):
                        b.op("tensor", lambda e, ob=ob, cb=cb, nch=nch, c0=c0:
                             e.matmul(
                                 PW1[:, nch * 512:nch * 512 + 512],
                                 WARE[:, WV_OFF + cb * 1024 + ob * 128:
                                      WV_OFF + cb * 1024 + ob * 128 + 128],
                                 XLN[:, cb * 2048 + c0:cb * 2048 + c0 + 512],
                                 start=(cb == 0), stop=(cb == 7)), inc="T")
                    for cb in range(CB):
                        b.op("tensor", lambda e, ob=ob, cb=cb, nch=nch:
                             e.matmul(
                                 PW2[:, nch * 512:nch * 512 + 512],
                                 WARE[:, WR_OFF + cb * 1024 + ob * 128:
                                      WR_OFF + cb * 1024 + ob * 128 + 128],
                                 A2[:, 8192 + cb * 1024 + nch * 512:
                                    8192 + cb * 1024 + nch * 512 + 512],
                                 start=(cb == 0), stop=(cb == 7)), inc="T")
                    tmk = b.mark("T")
                    b.wait("scalar", tmk)
                    if u >= 2:
                        b.wait("scalar", prod_mark[u - 2])
                    b.op("scalar", lambda e, nch=nch: e.activation(
                        TMP[:, nch * 512:nch * 512 + 512],
                        PW2[:, nch * 512:nch * 512 + 512], AF.Sigmoid),
                        inc="S")
                    sig_mark[u] = b.mark("S")
                    b.wait("vector", sig_mark[u])
                    b.op("vector", lambda e, nch=nch: e.tensor_mul(
                        TMP[:, 1024 + nch * 512:1024 + nch * 512 + 512],
                        TMP[:, nch * 512:nch * 512 + 512],
                        PW1[:, nch * 512:nch * 512 + 512]), inc="V")
                    b.wait("vector", b.mark("V"))
                    b.op("vector", lambda e, ob=ob, nch=nch, c0=c0:
                         e.scalar_tensor_tensor(
                             R[:, ob * 2048 + c0:ob * 2048 + c0 + 512],
                             TMP[:, 1024 + nch * 512:1024 + nch * 512 + 512],
                             0.5,
                             R[:, ob * 2048 + c0:ob * 2048 + c0 + 512],
                             ALU.mult, ALU.add), inc="V")
                    prod_mark[u] = b.mark("V")
                pw_free[0] = [sig_mark[15], prod_mark[15]]
                pw_free[1] = [sig_mark[15]]
                last_v = b.mark("V")
            return last_v

        def finish_dump(mark, src_ap, cast):
            eng = "gpsimd" if cast else "sync"
            b.wait(eng, mark)
            b.op(eng, lambda e: e.dma_start(
                out=y_d.rearrange("(cb p) t -> p cb t", p=128),
                in_=src_ap.rearrange("p (cb t) -> p cb t", cb=CB)),
                inc="DD", n=16)
            b.wait(eng, "DD")
            with nc.Block() as block:
                b.emit(block)

        xln1 = emit_ln([dg_xbf], [ds_x])
        if _DUMP == "xln1":
            finish_dump(xln1, XLN[:], True)
            return nc
        ffn1 = emit_ffn(dg_ffnw1, xln1, ("T", 0))
        if _DUMP == "ffn1":
            finish_dump(ffn1, R[:], False)
            return nc

        def emit_cast(r_mark, xln_war):
            b.wait("scalar", r_mark, *xln_war)
            b.wait("vector", r_mark, *xln_war)
            for cb in range(CB):
                if cb % 2 == 0:
                    b.op("scalar", lambda e, cb=cb: e.copy(
                        XLN[:, cb * 2048:cb * 2048 + 2048],
                        R[:, cb * 2048:cb * 2048 + 2048]), inc="S")
                else:
                    b.op("vector", lambda e, cb=cb: e.tensor_copy(
                        XLN[:, cb * 2048:cb * 2048 + 2048],
                        R[:, cb * 2048:cb * 2048 + 2048]), inc="V")
            return [b.mark("S"), b.mark("V")]

        cast2 = emit_cast(ffn1, [b.mark("T")])
        xln2 = emit_ln(cast2, [ffn1])
        if _DUMP == "xln2":
            finish_dump(xln2, XLN[:], True)
            return nc

        b.wait("tensor", *po_free)
        b.op("tensor", lambda e: e.matmul(
            PR1[0:1, 0:1], LVEC[:, 0:1], LVEC[:, 1:2], start=True, stop=False),
            inc="T")
        b.op("tensor", lambda e: e.matmul(
            PR1[0:1, 1:2], LVEC[:, 2:3], LVEC[:, 3:4], start=False, stop=True),
            inc="T")
        tlam = b.mark("T")
        b.wait("scalar", tlam)
        b.op("scalar", lambda e: e.activation(
            SC1[0:1, 0:2], PR1[0:1, 0:2], AF.Exp), inc="S")
        slam = b.mark("S")
        b.wait("vector", slam)
        b.op("vector", lambda e: e.reduce_sum(
            SC1[0:1, 2:3], SC1[0:1, 0:2], axis=AX.X), inc="V")
        b.wait("vector", b.mark("V"))
        b.op("vector", lambda e: e.tensor_scalar_add(
            SC1[0:1, 3:4], SC1[0:1, 2:3], -2.0 * LAMBDA_INIT), inc="V")
        vlam = b.mark("V")
        b.wait("tensor", vlam)
        b.op("tensor", lambda e: e.matmul(
            PO1[:, 0:1], ONESF[0:1, :], SC1[0:1, 3:4], start=True, stop=True),
            inc="T")
        tlam2 = b.mark("T")
        b.wait("vector", tlam2)
        b.op("vector", lambda e: e.tensor_copy(NLAM[:], PO1[:, 0:1]), inc="V")
        po_free = [b.mark("V")]

        wsrc = (wq0_d, wq1_d, wk0_d, wk1_d, wv_d)
        slice_war = [[], []]

        def emit_slice_dma(hp):
            ch = "DS0" if hp % 2 == 0 else "DS1"
            b.wait("gpsimd", *slice_war[hp % 2],
                   *([ffn1] if hp <= 1 else []))
            for widx in range(5):
                base = SLC[hp % 2] + widx * 1024
                b.op("gpsimd", lambda e, base=base, w=wsrc[widx], hp=hp:
                     e.dma_start(
                         out=WARE[:, base:base + 1024].rearrange(
                             "p (cb o) -> p cb o", cb=CB),
                         in_=w[hp].rearrange("(cb p) o -> p cb o", p=128)),
                     inc=ch, n=16)
            return b.mark(ch)

        slice_ready = {0: emit_slice_dma(0)}
        qk_war = []
        vtm_war = [p_vones]
        exp_free = [b.mark("S"), b.mark("S")]
        pt_free = {(j, p): b.mark("T") for j in range(2) for p in range(2)}
        PTJ = ((PT00, PT01), (PT10, PT11))

        b.wait("tensor", xln2)
        for hp in range(8):
            if hp + 1 < 8:
                slice_ready[hp + 1] = emit_slice_dma(hp + 1)
            b.wait("sync", *qk_war, *([ffn1] if hp == 0 else []))
            b.op("sync", lambda e, hp=hp: e.dma_start(
                out=WARE[64:KD, 4096:12288], in_=kaug_d[hp]), inc="DKA", n=16)
            kaug_mark = b.mark("DKA")

            b.wait("tensor", slice_ready[hp], *qk_war,
                   *exp_free, *pw_free[0], *pw_free[1])
            kevac = []
            for widx in (2, 3):
                j = widx - 2
                for half in range(2):
                    for cb in range(CB):
                        for nch in range(2):
                            b.op("tensor", lambda e, widx=widx, cb=cb, hp=hp,
                                 half=half, nch=nch: e.matmul(
                                     PW[half][:, nch * 512:nch * 512 + 512],
                                     slice_view(hp, widx, cb),
                                     XLN[:, cb * 2048 + half * 1024 + nch * 512:
                                         cb * 2048 + half * 1024 + nch * 512 + 512],
                                     start=(cb == 0), stop=(cb == 7)), inc="T")
                tmk = b.mark("T")
                for hh in range(2):
                    for half in range(2):
                        eng = "vector" if (hh + half) % 2 == 0 else "scalar"
                        b.wait(eng, tmk)
                        dst = ktile(j, hh)[0:64, half * 1024:half * 1024 + 1024]
                        src = PW[half][64 * hh:64 * hh + 64, 0:1024]
                        if eng == "vector":
                            b.op("vector", lambda e, d=dst, s=src:
                                 e.tensor_copy(d, s), inc="V")
                        else:
                            b.op("scalar", lambda e, d=dst, s=src:
                                 e.copy(d, s), inc="S")
                kevac += [b.mark("V"), b.mark("S")]
                b.wait("tensor", *kevac)
            b.wait("tensor", *vtm_war)
            vp_evac = {}
            for tt in range(16):
                par = tt % 2
                if tt >= 2:
                    b.wait("tensor", vp_evac[tt - 2])
                for cb in range(CB):
                    b.op("tensor", lambda e, tt=tt, cb=cb, hp=hp, par=par:
                         e.matmul(
                             PW[par][:, 0:128],
                             XLN[:, cb * 2048 + tt * 128:
                                 cb * 2048 + tt * 128 + 128],
                             slice_view(hp, 4, cb),
                             start=(cb == 0), stop=(cb == 7)), inc="T")
                tmk = b.mark("T")
                b.wait("vector", tmk)
                b.op("vector", lambda e, tt=tt, par=par: e.tensor_copy(
                    bass.AP(VTM, tt * 130, [[2080, 128], [65, 2], [1, 64]]),
                    PW[par][:, 0:128].rearrange("p (h c) -> p h c", h=2)),
                    inc="V")
                vp_evac[tt] = b.mark("V")
            vevac = [vp_evac[14], vp_evac[15]]

            for qh in range(2):
                b.wait("sync", *qk_war, *([ffn1] if hp == 0 else []))
                b.op("sync", lambda e, hp=hp, qh=qh: e.dma_start(
                    out=WARE[64:KD, 0:4096], in_=qaug_d[hp, qh]),
                    inc="DQA", n=16)
                qaug_mark = b.mark("DQA")
                b.wait("tensor", *qk_war, *exp_free, *vevac)
                for widx in (0, 1):
                    for cb in range(CB):
                        for nch in range(2):
                            b.op("tensor", lambda e, widx=widx, cb=cb, hp=hp,
                                 qh=qh, nch=nch: e.matmul(
                                     PW[widx][:, nch * 512:nch * 512 + 512],
                                     slice_view(hp, widx, cb),
                                     XLN[:, cb * 2048 + qh * 1024 + nch * 512:
                                         cb * 2048 + qh * 1024 + nch * 512 + 512],
                                     start=(cb == 0), stop=(cb == 7)), inc="T")
                tmk = b.mark("T")
                qevac = []
                for widx in (0, 1):
                    for hh in range(2):
                        eng = "vector" if hh == 0 else "scalar"
                        b.wait(eng, tmk)
                        dst = qtile(widx, hh)[0:64, :]
                        src = PW[widx][64 * hh:64 * hh + 64, 0:1024]
                        if eng == "vector":
                            b.op("vector", lambda e, d=dst, s=src:
                                 e.tensor_copy(d, s), inc="V")
                        else:
                            b.op("scalar", lambda e, d=dst, s=src:
                                 e.copy(d, s), inc="S")
                qevac = [b.mark("V"), b.mark("S")]
                if _DUMP == "proj0" and hp == 0 and qh == 0:
                    b.wait("gpsimd", *qevac, *kevac, *vevac,
                           kaug_mark, qaug_mark)
                    finish_dump(b.mark("P"), WARE[:, 0:16384], True)
                    return nc

                for hh in range(2):
                    DK = DTAB[2 * hp + hh]
                    ktmax = 8 if qh == 0 else 16
                    kt_lo = max(0, qh * 8 - DK + 1)
                    kt7 = max(0, qh * 8 + 8 - DK)
                    exp_done = {}
                    b.wait("tensor", kaug_mark, qaug_mark,
                           *qevac, *kevac, *vevac)

                    def emit_pv(j, kt, DK=DK, kt_lo=kt_lo, kt7=kt7,
                                ktmax=ktmax, hh=hh, qh=qh):
                        qc0 = max(0, kt * 128 - qh * 1024)
                        qt_hi = min(8, kt + DK - qh * 8)
                        b.wait("tensor", exp_done[(j, kt)])
                        if kt == kt_lo and j == 0:
                            b.wait("tensor", *po_free)
                        for qt in range(qc0 // 128, qt_hi):
                            dest = (PO[j][:, qt * 65:qt * 65 + 65]
                                    if qt < 7
                                    else PR1[:, j * 65:j * 65 + 65])
                            if qt < 7:
                                mm_start = (kt == kt_lo and qt == 0)
                                mm_stop = (kt == ktmax - 2 and qt == 6)
                            else:
                                mm_start = (kt == kt7 and j == 0)
                                mm_stop = (kt == ktmax - 1 and j == 1)
                            b.op("tensor", lambda e, qt=qt, kt=kt,
                                 hh=hh, dest=dest, j=j,
                                 mm_start=mm_start, mm_stop=mm_stop:
                                 e.matmul(
                                     dest,
                                     PTJ[j][kt % 2][:, qt * 128:qt * 128 + 128],
                                     VTM[:, kt * 130 + hh * 65:
                                         kt * 130 + hh * 65 + 65],
                                     start=mm_start, stop=mm_stop),
                                 inc="T")

                    for kt in range(kt_lo, ktmax):
                        qc0 = max(0, kt * 128 - qh * 1024)
                        qc_hi = min(1024, (kt + DK) * 128 - qh * 1024)
                        has_mask = kt * 128 >= qh * 1024
                        pieces = []
                        if qc0 < 512:
                            pieces.append((qc0, min(512, qc_hi),
                                           True, not has_mask))
                            if qc_hi > 512:
                                pieces.append((512, qc_hi, True, True))
                        else:
                            pieces.append((qc0, qc_hi, True, not has_mask))
                        for j in range(2):
                            b.wait("tensor", exp_free[j])
                            for (a0, a1, st, sp) in pieces:
                                b.op("tensor", lambda e, j=j, hh=hh, kt=kt,
                                     a0=a0, a1=a1, st=st, sp=sp:
                                     e.matmul(
                                         PW[j][:, a0:a1],
                                         ktile(j, hh)[:,
                                                      kt * 128:kt * 128 + 128],
                                         qtile(j, hh)[:, a0:a1],
                                         start=st, stop=sp), inc="T")
                            if has_mask:
                                b.op("tensor", lambda e, j=j, qc0=qc0:
                                     e.matmul(PW[j][:, qc0:qc0 + 128],
                                              IDEN[:], MASKT[:],
                                              start=False, stop=True), inc="T")
                            st_mark = b.mark("T")
                            b.wait("scalar", st_mark, pt_free[(j, kt % 2)])
                            b.op("scalar", lambda e, j=j, qc0=qc0, kt=kt,
                                 qc_hi=qc_hi: e.activation(
                                     PTJ[j][kt % 2][:, qc0:qc_hi],
                                     PW[j][:, qc0:qc_hi], AF.Exp,
                                     scale=SCALE), inc="S")
                            exp_done[(j, kt)] = b.mark("S")
                            exp_free[j] = exp_done[(j, kt)]
                        if kt > kt_lo:
                            emit_pv(0, kt - 1)
                            pt_free[(0, (kt - 1) % 2)] = b.mark("T")
                            emit_pv(1, kt - 1)
                            pt_free[(1, (kt - 1) % 2)] = b.mark("T")
                    emit_pv(0, ktmax - 1)
                    pt_free[(0, (ktmax - 1) % 2)] = b.mark("T")
                    emit_pv(1, ktmax - 1)
                    pt_free[(1, (ktmax - 1) % 2)] = b.mark("T")
                    pv_done = b.mark("T")

                    h = 2 * hp + hh
                    b.wait("vector", pv_done)

                    def ve(fn):
                        b.op("vector", fn, inc="V")
                        b.wait("vector", b.mark("V"))

                    def r_b(c0, n):
                        return bass.AP(RECIPS, c0, [[32, 128], [1, n], [0, 64]])

                    def mu_b(c0, n):
                        return bass.AP(MUVAR, c0, [[32, 128], [1, n], [0, 64]])
                    CSB_ = TMP[:, 0:512]
                    T2SB_ = TMP[:, 512:1024]
                    csb_v = bass.AP(TMP, 0, [[2048, 128], [64, 8], [1, 64]])
                    t2_v = bass.AP(TMP, 512, [[2048, 128], [64, 8], [1, 64]])
                    ve(lambda e: e.tensor_copy(ROWS[:, 0:455], PO1[:, 0:455]))
                    ve(lambda e: e.tensor_copy(ROWS[:, 520:975], PO2[:, 0:455]))
                    ve(lambda e: e.tensor_copy(
                        bass.AP(ROWS, 455, [[2048, 128], [520, 2], [1, 65]]),
                        PR1[:, 0:130].rearrange("p (j c) -> p j c", j=2)))
                    po_free = [b.mark("V")]
                    ve(lambda e: e.reciprocal(
                        RECIPS[:, 0:7], bass.AP(ROWS, 64, [[2048, 128], [65, 7]])))
                    ve(lambda e: e.reciprocal(RECIPS[:, 7:8], ROWS[:, 519:520]))
                    ve(lambda e: e.reciprocal(
                        RECIPS[:, 8:15],
                        bass.AP(ROWS, 584, [[2048, 128], [65, 7]])))
                    ve(lambda e: e.reciprocal(RECIPS[:, 15:16],
                                              ROWS[:, 1039:1040]))
                    ve(lambda e: e.tensor_scalar_mul(
                        RECIPS[:, 16:24], RECIPS[:, 8:16], NLAM[:, 0:1]))
                    ve(lambda e: e.tensor_tensor(
                        csb_v[:, 0:7, :],
                        bass.AP(ROWS, 0, [[2048, 128], [65, 7], [1, 64]]),
                        r_b(0, 7), ALU.mult))
                    ve(lambda e: e.tensor_tensor(
                        csb_v[:, 7:8, :],
                        bass.AP(ROWS, 455, [[2048, 128], [65, 1], [1, 64]]),
                        r_b(7, 1), ALU.mult))
                    ve(lambda e: e.tensor_tensor(
                        t2_v[:, 0:7, :],
                        bass.AP(ROWS, 520, [[2048, 128], [65, 7], [1, 64]]),
                        r_b(16, 7), ALU.mult))
                    ve(lambda e: e.tensor_tensor(
                        t2_v[:, 7:8, :],
                        bass.AP(ROWS, 975, [[2048, 128], [65, 1], [1, 64]]),
                        r_b(23, 1), ALU.mult))
                    ve(lambda e: e.tensor_add(CSB_, CSB_, T2SB_))
                    ve(lambda e: e.reduce_sum(
                        MUVAR[:, 0:8].rearrange("p (t o) -> p t o", t=8),
                        csb_v, axis=AX.X))
                    ve(lambda e: e.tensor_scalar_mul(
                        MUVAR[:, 0:8], MUVAR[:, 0:8], 1.0 / DH))
                    ve(lambda e: e.tensor_tensor(
                        t2_v, csb_v, mu_b(0, 8), ALU.subtract))
                    ve(lambda e: e.tensor_mul(CSB_, T2SB_, T2SB_))
                    ve(lambda e: e.reduce_sum(
                        MUVAR[:, 8:16].rearrange("p (t o) -> p t o", t=8),
                        csb_v, axis=AX.X))
                    vgn = b.mark("V")
                    b.wait("scalar", vgn, ("P", 2))
                    b.op("scalar", lambda e: e.activation(
                        MUVAR[:, 16:24], MUVAR[:, 8:16], AF.Ln,
                        bias=EPSC[:], scale=1.0 / DH), inc="S")
                    b.wait("scalar", b.mark("S"))
                    b.op("scalar", lambda e: e.activation(
                        MUVAR[:, 8:16], MUVAR[:, 16:24], AF.Exp, scale=-0.5),
                        inc="S")
                    sgn = b.mark("S")
                    b.wait("vector", sgn)
                    b.op("vector", lambda e, h=h, qh=qh: e.tensor_tensor(
                        bass.AP(A2, (qh * 8) * 1024 + h * 64,
                                [[16384, 128], [1024, 8], [1, 64]]),
                        t2_v, mu_b(8, 8), ALU.mult), inc="V")
                qk_war = [b.mark("T")]
            vtm_war = [b.mark("T")]
            slice_war[hp % 2] = [b.mark("T")]
        attn_core_V = b.mark("V")
        attn_core_T = b.mark("T")
        if _DUMP == "attncore":
            finish_dump(attn_core_V, A2[:], True)
            return nc

        b.wait("gpsimd", *slice_war[0], *slice_war[1])
        b.op("gpsimd", lambda e: e.dma_start(
            out=WARE[:, WO_OFF:WO_OFF + 8192].rearrange(
                "p (eb d) -> p eb d", eb=CB),
            in_=wo_d.rearrange("(eb p) d -> p eb d", p=128)), inc="DW", n=16)
        dg_wo = b.mark("DW")
        b.wait("tensor", dg_wo, dg_lnb, attn_core_T, *po_free)
        for db in range(CB):
            for eb in range(CB):
                b.op("tensor", lambda e, db=db, eb=eb: e.matmul(
                    PR1[:, db:db + 1],
                    WARE[:, WO_OFF + eb * 1024 + db * 128:
                         WO_OFF + eb * 1024 + db * 128 + 128],
                    LNBC[:, eb:eb + 1],
                    start=(eb == 0 and db == 0),
                    stop=(eb == 7 and db == 7)), inc="T")
        tbias = b.mark("T")
        b.wait("vector", tbias)
        b.op("vector", lambda e: e.tensor_copy(BIASD[:], PR1[:, 0:8]), inc="V")
        vbias = b.mark("V")
        b.wait("vector", tbias)
        for eb in range(CB):
            b.op("vector", lambda e, eb=eb: e.tensor_scalar_mul(
                WARE[:, WO_OFF + eb * 1024:WO_OFF + eb * 1024 + 1024],
                WARE[:, WO_OFF + eb * 1024:WO_OFF + eb * 1024 + 1024],
                LNWC[:, eb:eb + 1]), inc="V")
        p_wo = b.mark("V")
        b.wait("tensor", attn_core_V)
        tr_evac = {}
        for r in range(32):
            if r >= 1:
                b.wait("tensor", tr_evac[r - 1])
            for s in range(4):
                idx = r * 4 + s
                tt, eb = idx // 8, idx % 8
                b.op("tensor", lambda e, tt=tt, eb=eb, s=s: e.transpose(
                    PTR[:, s * 128:s * 128 + 128],
                    A2[:, tt * 1024 + eb * 128: tt * 1024 + eb * 128 + 128],
                    IDEN[:]), inc="T")
            tmk = b.mark("T")
            b.wait("vector", tmk)
            for s in range(4):
                idx = r * 4 + s
                tt, eb = idx // 8, idx % 8
                b.op("vector", lambda e, tt=tt, eb=eb, s=s: e.tensor_copy(
                    XLN[:, eb * 2048 + tt * 128: eb * 2048 + tt * 128 + 128],
                    PTR[:, s * 128:s * 128 + 128]), inc="V")
            tr_evac[r] = b.mark("V")
        tr_done = b.mark("V")
        b.wait("tensor", tr_done, p_wo)
        b.wait("vector", vbias)
        wo_evac = {}
        for db in range(CB):
            for half in range(2):
                k = db * 2 + half
                par = k % 2
                if k >= 2:
                    b.wait("tensor", wo_evac[k - 2])
                for eb in range(CB):
                    for nch in range(2):
                        b.op("tensor", lambda e, db=db, eb=eb, half=half,
                             nch=nch, par=par: e.matmul(
                                 PW[par][:, nch * 512:nch * 512 + 512],
                                 WARE[:, WO_OFF + eb * 1024 + db * 128:
                                      WO_OFF + eb * 1024 + db * 128 + 128],
                                 XLN[:, eb * 2048 + half * 1024 + nch * 512:
                                     eb * 2048 + half * 1024 + nch * 512 + 512],
                                 start=(eb == 0), stop=(eb == 7)), inc="T")
                tmk = b.mark("T")
                b.wait("vector", tmk)
                b.op("vector", lambda e, db=db, half=half, par=par:
                     e.scalar_tensor_tensor(
                         R[:, db * 2048 + half * 1024:
                           db * 2048 + half * 1024 + 1024],
                         R[:, db * 2048 + half * 1024:
                           db * 2048 + half * 1024 + 1024],
                         1.0, PW[par][:, 0:1024], ALU.mult, ALU.add), inc="V")
                b.wait("vector", b.mark("V"))
                b.op("vector", lambda e, db=db, half=half: e.tensor_scalar_add(
                    R[:, db * 2048 + half * 1024:
                      db * 2048 + half * 1024 + 1024],
                    R[:, db * 2048 + half * 1024:
                      db * 2048 + half * 1024 + 1024],
                    BIASD[:, db:db + 1]), inc="V")
                wo_evac[k] = b.mark("V")
        pw_free[0] = [wo_evac[14]]
        pw_free[1] = [wo_evac[15]]
        attn_done = b.mark("V")

        if _DUMP == "attn":
            finish_dump(attn_done, R[:], False)
            return nc
        b.wait("gpsimd", attn_done)
        for off, src in ((WK_OFF, wkt_d), (WV_OFF, wvt_d), (WR_OFF, wrt_d)):
            b.op("gpsimd", lambda e, o=off, s=src: e.dma_start(
                out=WARE[:, o:o + 8192].rearrange("p (cb o) -> p cb o", cb=CB),
                in_=s.rearrange("(cb p) o -> p cb o", p=128)), inc="DW", n=16)
        dg_ffnw2 = b.mark("DW")
        cast3 = emit_cast(attn_done, [b.mark("T")])
        xln3 = emit_ln(cast3, [attn_done])
        if _DUMP == "xln3":
            finish_dump(xln3, XLN[:], True)
            return nc
        ffn2 = emit_ffn(dg_ffnw2, xln3, tr_done)

        b.wait("sync", ffn2)
        b.op("sync", lambda e: e.dma_start(
            out=y_d.rearrange("(cb p) t -> p cb t", p=128),
            in_=R[:].rearrange("p (cb t) -> p cb t", cb=CB)), inc="DX", n=16)
        b.wait("sync", "DX")

        with nc.Block() as block:
            b.emit(block)
    return nc


def _host_inputs(ins):
    f32 = np.float32
    sh = {}
    for name, key in (("wq0", "L0_attn_wq0"), ("wq1", "L0_attn_wq1"),
                      ("wk0", "L0_attn_wk0"), ("wk1", "L0_attn_wk1"),
                      ("wv", "L0_attn_wv")):
        w = np.asarray(ins[key], f32)
        sh[name] = np.ascontiguousarray(
            w.reshape(D, 8, 128).transpose(1, 0, 2))
    sh["wo"] = np.ascontiguousarray(np.asarray(ins["L0_attn_wo"], f32))
    sh["wrt"] = np.ascontiguousarray(np.asarray(ins["L0_ffn_W_r"], f32).T)
    sh["wkt"] = np.ascontiguousarray(np.asarray(ins["L0_ffn_W_k"], f32).T)
    sh["wvt"] = np.ascontiguousarray(np.asarray(ins["L0_ffn_W_v"], f32).T)
    sh["mur"] = np.ascontiguousarray(
        np.asarray(ins["L0_ffn_mu_r"], f32).reshape(8, 128).T)
    sh["muk"] = np.ascontiguousarray(
        np.asarray(ins["L0_ffn_mu_k"], f32).reshape(8, 128).T)
    sh["lnw"] = np.ascontiguousarray(
        np.asarray(ins["L0_attn_ln_w"], f32).reshape(-1).reshape(8, 128).T)
    sh["lnb"] = np.ascontiguousarray(
        np.asarray(ins["L0_attn_ln_b"], f32).reshape(-1).reshape(8, 128).T)
    lv = np.stack([np.asarray(ins["L0_attn_l0"], f32),
                   np.asarray(ins["L0_attn_l1"], f32),
                   np.asarray(ins["L0_attn_l2"], f32),
                   np.asarray(ins["L0_attn_l3"], f32)], axis=1)
    sh["lvec"] = np.ascontiguousarray(lv)

    kk, qq = np.meshgrid(np.arange(128), np.arange(128), indexing="ij")
    sh["maskt"] = np.where(kk > qq, -1e30, 0.0).astype(BF)
    sh["iden"] = np.eye(128).astype(BF)
    sh["onesc"] = np.ones((128, 1), BF)
    sh["onesf"] = np.ones((65, 128), f32)
    slopes = _slopes()
    t_idx = np.arange(T, dtype=np.float64)
    kaug = np.zeros((8, 7, 8192), np.float64)
    for hp in range(8):
        for blk in range(4):
            h = 2 * hp + blk // 2
            ks = _bf16_splits(8.0 * slopes[h] * t_idx, NKR)
            kaug[hp, 0:NKR, blk * 2048:(blk + 1) * 2048] = ks
            kaug[hp, NKR:7, blk * 2048:(blk + 1) * 2048] = 1.0
    sh["kaug"] = kaug.astype(BF)
    qaug = np.zeros((8, 2, 7, 4096), np.float64)
    for hp in range(8):
        for qh in range(2):
            for blk in range(4):
                h = 2 * hp + blk // 2
                qs = _bf16_splits(
                    -8.0 * slopes[h] * (qh * 1024 + np.arange(1024)), NQR)
                qaug[hp, qh, 0:NKR, blk * 1024:(blk + 1) * 1024] = 1.0
                qaug[hp, qh, NKR:7, blk * 1024:(blk + 1) * 1024] = qs
    sh["qaug"] = qaug.astype(BF)
    return sh


def kernel(**inputs):
    ins = {k: np.asarray(v) for k, v in inputs.items()}
    sh = _host_inputs(ins)
    x = np.asarray(ins["L0_attn_inputs"], np.float32)
    nc = build_nc()
    in_maps = []
    for bidx in range(8):
        m = dict(sh)
        m["x"] = np.ascontiguousarray(x[bidx].T)
        in_maps.append(m)
    res = run_bass_kernel_spmd(nc, in_maps, core_ids=list(range(8)))
    return np.stack([np.asarray(res.results[i]["y"]).T for i in range(8)])


if __name__ == "__main__":
    import reference as ref
    ins = ref.setup_inputs()
    out = kernel(**{k: np.asarray(v) for k, v in ins.items()})
    expect = np.asarray(ref.reference(**ins))
    err = np.abs(out - expect)
    rel = np.sqrt(((out - expect) ** 2).mean() / (expect ** 2).mean())
    print("rel_l2", rel, "max_abs", err.max())


# version 20 (final)
# speedup vs baseline: 1.0586x; 1.0270x over previous
import numpy as np
import ml_dtypes
from contextlib import ExitStack
import concourse.bass as bass
import concourse.mybir as mybir
from concourse.bass_utils import run_bass_kernel_spmd

F32 = mybir.dt.float32
BF16 = mybir.dt.bfloat16
AF = mybir.ActivationFunctionType
ALU = mybir.AluOpType
AX = mybir.AxisListType
BF = ml_dtypes.bfloat16

T = 2048
D = 1024
H = 16
DH = 64
CB = 8
NKR, NQR = 5, 2
NAUG = NKR + NQR
KD = DH + NAUG
SCALE = 0.125
EPS = 1e-5
LAMBDA_INIT = 0.8


def _bf16_splits(v, n):
    parts, r = [], np.asarray(v, np.float64).copy()
    for _ in range(n):
        p = r.astype(BF).astype(np.float64)
        parts.append(p)
        r = r - p
    return np.stack(parts)


def _slopes():
    return 2.0 ** (-8.0 * (np.arange(1, H + 1, dtype=np.float64)) / H)


TAU = 30.0
DTAB = [min(16, int((np.ceil(TAU / s) + 127) // 128) + 1) for s in _slopes()]


class Bld:
    ENGS = ("sync", "gpsimd", "tensor", "scalar", "vector")

    def __init__(self, nc, stack):
        self.nc = nc
        self.prog = {e: [] for e in self.ENGS}
        self.ev = {}
        for name in ("T", "S", "V", "P", "DX", "DC", "DXB", "DW",
                     "DS0", "DS1", "DKA", "DQA", "DD", "DL"):
            sem = stack.enter_context(nc.semaphore("ev" + name))
            self.ev[name] = [sem, 0]

    def wait(self, eng, *marks):
        seen = {}
        for m in marks:
            if isinstance(m, str):
                name, k = m, self.ev[m][1]
            else:
                name, k = m
            if k > seen.get(name, 0):
                seen[name] = k
        for name, k in seen.items():
            sem = self.ev[name][0]
            self.prog[eng].append(lambda e, s=sem, kk=k: e.wait_ge(s, kk))

    def op(self, eng, fn, inc=None, n=1):
        if inc is None:
            self.prog[eng].append(fn)
        else:
            rec = self.ev[inc]
            rec[1] += n
            self.prog[eng].append(
                lambda e, fn=fn, s=rec[0], nn=n: fn(e).then_inc(s, nn))

    def mark(self, name):
        return (name, self.ev[name][1])

    def emit(self, block):
        for eng in self.ENGS:
            ops = self.prog[eng]
            if ops:
                getattr(block, eng)(
                    lambda engine, ops=ops: [f(engine) for f in ops])


_DUMP = None


def build_nc():
    nc = bass.Bass("TRN2", target_bir_lowering=False)
    dt = nc.dram_tensor
    x_d = dt("x", [D, T], F32, kind="ExternalInput").ap()
    y_d = dt("y", [D, T], F32, kind="ExternalOutput").ap()
    wq0_d = dt("wq0", [8, D, 128], F32, kind="ExternalInput").ap()
    wq1_d = dt("wq1", [8, D, 128], F32, kind="ExternalInput").ap()
    wk0_d = dt("wk0", [8, D, 128], F32, kind="ExternalInput").ap()
    wk1_d = dt("wk1", [8, D, 128], F32, kind="ExternalInput").ap()
    wv_d = dt("wv", [8, D, 128], F32, kind="ExternalInput").ap()
    wo_d = dt("wo", [D, D], F32, kind="ExternalInput").ap()
    wrt_d = dt("wrt", [D, D], F32, kind="ExternalInput").ap()
    wkt_d = dt("wkt", [D, D], F32, kind="ExternalInput").ap()
    wvt_d = dt("wvt", [D, D], F32, kind="ExternalInput").ap()
    mur_d = dt("mur", [128, 8], F32, kind="ExternalInput").ap()
    muk_d = dt("muk", [128, 8], F32, kind="ExternalInput").ap()
    lnw_d = dt("lnw", [128, 8], F32, kind="ExternalInput").ap()
    lnb_d = dt("lnb", [128, 8], F32, kind="ExternalInput").ap()
    lv_d = dt("lvec", [64, 4], F32, kind="ExternalInput").ap()
    iden_d = dt("iden", [128, 128], BF16, kind="ExternalInput").ap()
    mask_d = dt("maskt", [128, 128], BF16, kind="ExternalInput").ap()
    onesc_d = dt("onesc", [128, 1], BF16, kind="ExternalInput").ap()
    onesf_d = dt("onesf", [65, 128], F32, kind="ExternalInput").ap()
    kaug_d = dt("kaug", [8, 7, 8192], BF16, kind="ExternalInput").ap()
    qaug_d = dt("qaug", [8, 2, 7, 4096], BF16, kind="ExternalInput").ap()

    with ExitStack() as st:
        e = st.enter_context
        R = e(nc.sbuf_tensor("R", [128, 16384], F32))
        XLN = e(nc.sbuf_tensor("XLN", [128, 16384], BF16))
        A2 = e(nc.sbuf_tensor("A2", [128, 16384], BF16))
        WARE = e(nc.sbuf_tensor("WARE", [128, 24576], BF16))
        VTM = e(nc.sbuf_tensor("VTM", [128, 2080], BF16))
        PT00 = e(nc.sbuf_tensor("PT00", [128, 1024], BF16))
        PT01 = e(nc.sbuf_tensor("PT01", [128, 1024], BF16))
        PT10 = e(nc.sbuf_tensor("PT10", [128, 1024], BF16))
        PT11 = e(nc.sbuf_tensor("PT11", [128, 1024], BF16))
        ROWS = e(nc.sbuf_tensor("ROWS", [128, 2048], F32))
        TMP = e(nc.sbuf_tensor("TMP", [128, 2048], F32))
        SQS = e(nc.sbuf_tensor("SQS", [128, 1024], BF16))
        RECIPS = e(nc.sbuf_tensor("RECIPS", [128, 32], F32))
        MUVAR = e(nc.sbuf_tensor("MUVAR", [128, 32], F32))
        BND = e(nc.sbuf_tensor("BND", [128, 8], BF16))
        IDEN = e(nc.sbuf_tensor("IDEN", [128, 128], BF16))
        MASKT = e(nc.sbuf_tensor("MASKT", [128, 128], BF16))
        ONESC = e(nc.sbuf_tensor("ONESC", [128, 1], BF16))
        ONESF = e(nc.sbuf_tensor("ONESF", [65, 128], F32))
        MURC = e(nc.sbuf_tensor("MURC", [128, 8], F32))
        MUKC = e(nc.sbuf_tensor("MUKC", [128, 8], F32))
        MURB = e(nc.sbuf_tensor("MURB", [128, 8], F32))
        MUKB = e(nc.sbuf_tensor("MUKB", [128, 8], F32))
        LNWC = e(nc.sbuf_tensor("LNWC", [128, 8], F32))
        LNBC = e(nc.sbuf_tensor("LNBC", [128, 8], BF16))
        LVEC = e(nc.sbuf_tensor("LVEC", [64, 4], F32))
        NLAM = e(nc.sbuf_tensor("NLAM", [128, 1], F32))
        BIASD = e(nc.sbuf_tensor("BIASD", [128, 8], F32))
        SC1 = e(nc.sbuf_tensor("SC1", [1, 8], F32))
        EPSC = e(nc.sbuf_tensor("EPSC", [128, 1], F32))
        PW1 = e(nc.psum_tensor("PW1", [128, 1024], F32))
        PW2 = e(nc.psum_tensor("PW2", [128, 1024], F32))
        PO1 = e(nc.psum_tensor("PO1", [128, 512], F32))
        PO2 = e(nc.psum_tensor("PO2", [128, 512], F32))
        PR1 = e(nc.psum_tensor("PR1", [128, 512], F32))
        PTR = e(nc.psum_tensor("PTR", [128, 512], BF16))
        b = Bld(nc, st)
        PW = (PW1, PW2)
        PO = (PO1, PO2)

        pw_free = [[], []]
        po_free = []

        WK_OFF, WV_OFF, WR_OFF = 0, 8192, 16384

        def qtile(j, hh):
            return WARE[0:KD, (j + 2 * hh) * 1024:(j + 2 * hh) * 1024 + 1024]

        def ktile(j, hh):
            off = 4096 + (j + 2 * hh) * 2048
            return WARE[0:KD, off:off + 2048]

        SLC = (12288, 17408)
        WO_OFF = 12288

        def slice_view(hp, widx, cbk):
            base = SLC[hp % 2] + widx * 1024
            return WARE[:, base + cbk * 128: base + cbk * 128 + 128]

        x_r = x_d.rearrange("(cb p) t -> p cb t", p=128)
        b.op("sync", lambda e: e.dma_start(
            out=R[:].rearrange("p (cb t) -> p cb t", cb=CB), in_=x_r),
            inc="DX", n=16)
        ds_x = b.mark("DX")
        for (dst, src) in ((IDEN, iden_d), (MASKT, mask_d), (ONESC, onesc_d),
                           (ONESF, onesf_d), (MURC, mur_d), (MUKC, muk_d),
                           (LNWC, lnw_d), (LVEC, lv_d)):
            b.op("sync", lambda e, d=dst, s=src: e.dma_start(out=d[:], in_=s),
                 inc="DC", n=16)
        ds_consts = b.mark("DC")
        b.op("gpsimd", lambda e: e.dma_start(out=LNBC[:], in_=lnb_d),
             inc="DL", n=16)
        dg_lnb = b.mark("DL")
        xchunk = []
        for g, ch in enumerate(("DXB", "DKA", "DQA", "DD")):
            b.op("gpsimd", lambda e, g=g: e.dma_start(
                out=bass.AP(XLN, g * 512, [[16384, 128], [2048, 8], [1, 512]]),
                in_=bass.AP(x_d.tensor, g * 512,
                            [[2048, 128], [262144, 8], [1, 512]])),
                inc=ch, n=16)
            xchunk.append(b.mark(ch))
        for off, src in ((WK_OFF, wkt_d), (WV_OFF, wvt_d), (WR_OFF, wrt_d)):
            b.op("gpsimd", lambda e, o=off, s=src: e.dma_start(
                out=WARE[:, o:o + 8192].rearrange("p (cb o) -> p cb o", cb=CB),
                in_=s.rearrange("(cb p) o -> p cb o", p=128)), inc="DW", n=16)
        dg_ffnw1 = b.mark("DW")
        b.wait("gpsimd", ds_consts)
        b.op("gpsimd", lambda e: e.memset(
            bass.AP(VTM, 64, [[2080, 128], [65, 32]]), 1.0), inc="P")
        b.op("gpsimd", lambda e: e.memset(EPSC[:], EPS), inc="P")
        p_vones = b.mark("P")
        b.wait("vector", ds_consts)
        b.op("vector", lambda e: e.tensor_scalar(
            MURB[:], MURC[:], -1.0, 1.0, ALU.mult, ALU.add), inc="V")
        b.op("vector", lambda e: e.tensor_scalar(
            MUKB[:], MUKC[:], -1.0, 1.0, ALU.mult, ALU.add), inc="V")
        v_params = b.mark("V")

        def emit_ln(xln_ready_g, r_marks):
            sq_free = [b.mark("T"), b.mark("T")]
            b.wait("tensor", ds_consts,
                   *pw_free[0], *pw_free[1], *po_free)
            for g in range(4):
                b.wait("vector", *xln_ready_g[g])
                b.wait("tensor", *xln_ready_g[g])
                for cb in range(CB):
                    par = cb % 2
                    xv = XLN[:, cb * 2048 + g * 512: cb * 2048 + g * 512 + 512]
                    b.wait("vector", sq_free[par])
                    b.op("vector", lambda e, xv=xv, par=par: e.tensor_mul(
                        SQS[:, par * 512:par * 512 + 512], xv, xv), inc="V")
                    vmk = b.mark("V")
                    b.wait("tensor", vmk)
                    b.op("tensor", lambda e, xv=xv, cb=cb: e.matmul(
                        PR1[0:1, 0:512], ONESC[:], xv,
                        start=(cb == 0), stop=(cb == 7)), inc="T")
                    b.op("tensor", lambda e, par=par, cb=cb: e.matmul(
                        PO1[0:1, 0:512], ONESC[:],
                        SQS[:, par * 512:par * 512 + 512],
                        start=(cb == 0), stop=(cb == 7)), inc="T")
                    sq_free[par] = b.mark("T")
                tmk = b.mark("T")
                b.wait("vector", tmk)
                b.op("vector", lambda e, g=g: e.tensor_copy(
                    ROWS[0:1, g * 512:g * 512 + 512], PR1[0:1, 0:512]), inc="V")
                b.op("vector", lambda e, g=g: e.tensor_copy(
                    ROWS[32:33, g * 512:g * 512 + 512], PO1[0:1, 0:512]), inc="V")
                b.wait("tensor", b.mark("V"))
            def sev(fn):
                b.op("vector", fn, inc="V")
                b.wait("vector", b.mark("V"))

            def ses(fn):
                b.op("scalar", fn, inc="S")
                b.wait("scalar", b.mark("S"))
            b.wait("scalar", b.mark("V"), ("P", 2))
            b.wait("vector", b.mark("V"))
            ses(lambda e: e.activation(
                TMP[32:33, :], ROWS[0:1, :], AF.Square, scale=1.0 / D))
            smsq = b.mark("S")
            b.wait("vector", smsq)
            sev(lambda e: e.tensor_scalar_mul(ROWS[0:1, :], ROWS[0:1, :], 1.0 / D))
            sev(lambda e: e.scalar_tensor_tensor(
                ROWS[32:33, :], ROWS[32:33, :], 1.0 / D, TMP[32:33, :],
                ALU.mult, ALU.subtract))
            vrow = b.mark("V")
            b.wait("scalar", vrow)
            ses(lambda e: e.activation(
                ROWS[64:65, :], ROWS[32:33, :], AF.Ln, bias=EPSC[0:1, :],
                scale=1.0))
            ses(lambda e: e.activation(
                ROWS[32:33, :], ROWS[64:65, :], AF.Exp, scale=-0.5))
            srow = b.mark("S")
            b.wait("tensor", srow, vrow)
            b.wait("vector", *r_marks)
            for g in range(4):
                b.op("tensor", lambda e, g=g: e.matmul(
                    PO1[:, 0:512], ONESF[0:1, :],
                    ROWS[0:1, g * 512:g * 512 + 512], start=True, stop=True),
                    inc="T")
                b.op("tensor", lambda e, g=g: e.matmul(
                    PO2[:, 0:512], ONESF[32:33, :],
                    ROWS[32:33, g * 512:g * 512 + 512], start=True, stop=True),
                    inc="T")
                tmk = b.mark("T")
                b.wait("vector", tmk)
                for cb in range(CB):
                    par = cb % 2
                    off = cb * 2048 + g * 512
                    b.op("vector", lambda e, off=off, par=par: e.tensor_sub(
                        TMP[:, par * 512:par * 512 + 512],
                        R[:, off:off + 512], PO1[:, 0:512]), inc="V")
                    b.wait("vector", b.mark("V"))
                    b.op("vector", lambda e, off=off, par=par: e.tensor_mul(
                        XLN[:, off:off + 512],
                        TMP[:, par * 512:par * 512 + 512], PO2[:, 0:512]),
                        inc="V")
                vmk = b.mark("V")
                b.wait("tensor", vmk)
            return b.mark("V")

        def emit_ffn(dg_w_mark, xln_mark, a2_free):
            nonlocal pw_free
            last_v = xln_mark
            for qh in range(2):
                b.wait("vector", xln_mark, a2_free, v_params, "S")
                stt_done = [None, None]
                mi = 0
                for (ydst, ac, bcol) in ((0, MUKC, MUKB), (8192, MURC, MURB)):
                    for cb in range(CB):
                        sl = mi % 2
                        mi += 1
                        to = sl * 1024
                        xc = XLN[:, cb * 2048 + qh * 1024:
                                 cb * 2048 + qh * 1024 + 1024]
                        yv = A2[:, ydst + cb * 1024: ydst + cb * 1024 + 1024]
                        if stt_done[sl] is not None:
                            b.wait("vector", stt_done[sl])
                        b.op("vector", lambda e, xc=xc, bcol=bcol, cb=cb, to=to:
                             e.tensor_scalar_mul(TMP[:, to:to + 1023],
                                                 xc[:, 0:1023],
                                                 bcol[:, cb:cb + 1]), inc="V")
                        if qh == 1:
                            b.op("vector", lambda e, bcol=bcol, cb=cb, to=to:
                                 e.tensor_scalar_mul(TMP[:, to + 1023:to + 1024],
                                                     BND[:, cb:cb + 1],
                                                     bcol[:, cb:cb + 1]), inc="V")
                        b.wait("vector", b.mark("V"))
                        b.op("vector", lambda e, xc=xc, ac=ac, cb=cb, yv=yv,
                             to=to: e.scalar_tensor_tensor(
                                 yv[:, 1:1024], xc[:, 1:1024], ac[:, cb:cb + 1],
                                 TMP[:, to:to + 1023], ALU.mult, ALU.add),
                             inc="V")
                        if qh == 0:
                            b.op("vector", lambda e, xc=xc, ac=ac, cb=cb, yv=yv:
                                 e.tensor_scalar_mul(yv[:, 0:1], xc[:, 0:1],
                                                     ac[:, cb:cb + 1]), inc="V")
                        else:
                            b.op("vector", lambda e, xc=xc, ac=ac, cb=cb, yv=yv,
                                 to=to: e.scalar_tensor_tensor(
                                     yv[:, 0:1], xc[:, 0:1], ac[:, cb:cb + 1],
                                     TMP[:, to + 1023:to + 1024],
                                     ALU.mult, ALU.add), inc="V")
                        stt_done[sl] = b.mark("V")
                if qh == 0:
                    b.op("vector", lambda e: e.tensor_copy(
                        BND[:], bass.AP(XLN, 1023, [[16384, 128], [2048, 8]])),
                        inc="V")
                    b.wait("vector", b.mark("V"))
                ymk = b.mark("V")

                b.wait("tensor", ymk, dg_w_mark, *pw_free[0], *pw_free[1])
                relu_mark, sq_mark = {}, {}
                for ob in range(CB):
                    par = ob % 2
                    if ob >= 2:
                        b.wait("tensor", relu_mark[ob - 2])
                    for cb in range(CB):
                        for nch in range(2):
                            b.op("tensor", lambda e, ob=ob, cb=cb, nch=nch,
                                 par=par, qh=qh: e.matmul(
                                     PW[par][:, nch * 512:nch * 512 + 512],
                                     WARE[:, WK_OFF + cb * 1024 + ob * 128:
                                          WK_OFF + cb * 1024 + ob * 128 + 128],
                                     A2[:, cb * 1024 + nch * 512:
                                        cb * 1024 + nch * 512 + 512],
                                     start=(cb == 0), stop=(cb == 7)), inc="T")
                    tmk = b.mark("T")
                    b.wait("scalar", tmk)
                    if ob >= 2:
                        b.wait("scalar", sq_mark[ob - 2])
                    b.op("scalar", lambda e, par=par: e.activation(
                        TMP[:, par * 1024:par * 1024 + 1024],
                        PW[par][:, 0:1024], AF.Relu), inc="S")
                    relu_mark[ob] = b.mark("S")
                    b.wait("vector", relu_mark[ob])
                    b.op("vector", lambda e, ob=ob, par=par, qh=qh: e.tensor_mul(
                        XLN[:, ob * 2048 + qh * 1024:
                            ob * 2048 + qh * 1024 + 1024],
                        TMP[:, par * 1024:par * 1024 + 1024],
                        TMP[:, par * 1024:par * 1024 + 1024]), inc="V")
                    sq_mark[ob] = b.mark("V")
                kact_mark = b.mark("V")

                b.wait("tensor", kact_mark, relu_mark[6], relu_mark[7])
                prod_mark, sig_mark = {}, {}
                for u in range(2 * CB):
                    ob, nch = u // 2, u % 2
                    c0 = qh * 1024 + nch * 512
                    if u >= 2:
                        b.wait("tensor", prod_mark[u - 2])
                    for cb in range(CB):
                        b.op("tensor", lambda e, ob=ob, cb=cb, nch=nch, c0=c0:
                             e.matmul(
                                 PW1[:, nch * 512:nch * 512 + 512],
                                 WARE[:, WV_OFF + cb * 1024 + ob * 128:
                                      WV_OFF + cb * 1024 + ob * 128 + 128],
                                 XLN[:, cb * 2048 + c0:cb * 2048 + c0 + 512],
                                 start=(cb == 0), stop=(cb == 7)), inc="T")
                    for cb in range(CB):
                        b.op("tensor", lambda e, ob=ob, cb=cb, nch=nch:
                             e.matmul(
                                 PW2[:, nch * 512:nch * 512 + 512],
                                 WARE[:, WR_OFF + cb * 1024 + ob * 128:
                                      WR_OFF + cb * 1024 + ob * 128 + 128],
                                 A2[:, 8192 + cb * 1024 + nch * 512:
                                    8192 + cb * 1024 + nch * 512 + 512],
                                 start=(cb == 0), stop=(cb == 7)), inc="T")
                    tmk = b.mark("T")
                    b.wait("scalar", tmk)
                    if u >= 2:
                        b.wait("scalar", prod_mark[u - 2])
                    b.op("scalar", lambda e, nch=nch: e.activation(
                        TMP[:, nch * 512:nch * 512 + 512],
                        PW2[:, nch * 512:nch * 512 + 512], AF.Sigmoid),
                        inc="S")
                    sig_mark[u] = b.mark("S")
                    b.wait("vector", sig_mark[u])
                    b.op("vector", lambda e, nch=nch: e.tensor_mul(
                        TMP[:, 1024 + nch * 512:1024 + nch * 512 + 512],
                        TMP[:, nch * 512:nch * 512 + 512],
                        PW1[:, nch * 512:nch * 512 + 512]), inc="V")
                    b.wait("vector", b.mark("V"))
                    b.op("vector", lambda e, ob=ob, nch=nch, c0=c0:
                         e.scalar_tensor_tensor(
                             R[:, ob * 2048 + c0:ob * 2048 + c0 + 512],
                             TMP[:, 1024 + nch * 512:1024 + nch * 512 + 512],
                             0.5,
                             R[:, ob * 2048 + c0:ob * 2048 + c0 + 512],
                             ALU.mult, ALU.add), inc="V")
                    prod_mark[u] = b.mark("V")
                pw_free[0] = [sig_mark[15], prod_mark[15]]
                pw_free[1] = [sig_mark[15]]
                last_v = b.mark("V")
            return last_v

        def finish_dump(mark, src_ap, cast):
            eng = "gpsimd" if cast else "sync"
            b.wait(eng, mark)
            b.op(eng, lambda e: e.dma_start(
                out=y_d.rearrange("(cb p) t -> p cb t", p=128),
                in_=src_ap.rearrange("p (cb t) -> p cb t", cb=CB)),
                inc="DD", n=16)
            b.wait(eng, "DD")
            with nc.Block() as block:
                b.emit(block)

        xln1 = emit_ln([[m] for m in xchunk], [ds_x])
        if _DUMP == "xln1":
            finish_dump(xln1, XLN[:], True)
            return nc
        ffn1 = emit_ffn(dg_ffnw1, xln1, ("T", 0))
        if _DUMP == "ffn1":
            finish_dump(ffn1, R[:], False)
            return nc

        def emit_cast(r_mark, xln_war):
            b.wait("scalar", r_mark, *xln_war)
            b.wait("vector", r_mark, *xln_war)
            for cb in range(CB):
                if cb % 2 == 0:
                    b.op("scalar", lambda e, cb=cb: e.copy(
                        XLN[:, cb * 2048:cb * 2048 + 2048],
                        R[:, cb * 2048:cb * 2048 + 2048]), inc="S")
                else:
                    b.op("vector", lambda e, cb=cb: e.tensor_copy(
                        XLN[:, cb * 2048:cb * 2048 + 2048],
                        R[:, cb * 2048:cb * 2048 + 2048]), inc="V")
            return [b.mark("S"), b.mark("V")]

        cast2 = emit_cast(ffn1, [b.mark("T")])
        xln2 = emit_ln([cast2] * 4, [ffn1])
        if _DUMP == "xln2":
            finish_dump(xln2, XLN[:], True)
            return nc

        b.wait("tensor", *po_free)
        b.op("tensor", lambda e: e.matmul(
            PR1[0:1, 0:1], LVEC[:, 0:1], LVEC[:, 1:2], start=True, stop=False),
            inc="T")
        b.op("tensor", lambda e: e.matmul(
            PR1[0:1, 1:2], LVEC[:, 2:3], LVEC[:, 3:4], start=False, stop=True),
            inc="T")
        tlam = b.mark("T")
        b.wait("scalar", tlam)
        b.op("scalar", lambda e: e.activation(
            SC1[0:1, 0:2], PR1[0:1, 0:2], AF.Exp), inc="S")
        slam = b.mark("S")
        b.wait("vector", slam)
        b.op("vector", lambda e: e.reduce_sum(
            SC1[0:1, 2:3], SC1[0:1, 0:2], axis=AX.X), inc="V")
        b.wait("vector", b.mark("V"))
        b.op("vector", lambda e: e.tensor_scalar_add(
            SC1[0:1, 3:4], SC1[0:1, 2:3], -2.0 * LAMBDA_INIT), inc="V")
        vlam = b.mark("V")
        b.wait("tensor", vlam)
        b.op("tensor", lambda e: e.matmul(
            PO1[:, 0:1], ONESF[0:1, :], SC1[0:1, 3:4], start=True, stop=True),
            inc="T")
        tlam2 = b.mark("T")
        b.wait("vector", tlam2)
        b.op("vector", lambda e: e.tensor_copy(NLAM[:], PO1[:, 0:1]), inc="V")
        po_free = [b.mark("V")]

        wsrc = (wq0_d, wq1_d, wk0_d, wk1_d, wv_d)
        slice_war = [[], []]

        def emit_slice_dma(hp):
            ch = "DS0" if hp % 2 == 0 else "DS1"
            b.wait("gpsimd", *slice_war[hp % 2],
                   *([ffn1] if hp <= 1 else []))
            for widx in range(5):
                base = SLC[hp % 2] + widx * 1024
                b.op("gpsimd", lambda e, base=base, w=wsrc[widx], hp=hp:
                     e.dma_start(
                         out=WARE[:, base:base + 1024].rearrange(
                             "p (cb o) -> p cb o", cb=CB),
                         in_=w[hp].rearrange("(cb p) o -> p cb o", p=128)),
                     inc=ch, n=16)
            return b.mark(ch)

        slice_ready = {0: emit_slice_dma(0)}
        qk_war = []
        vtm_war = [p_vones]
        exp_free = [b.mark("S"), b.mark("S")]
        pt_free = {(j, p): b.mark("T") for j in range(2) for p in range(2)}
        PTJ = ((PT00, PT01), (PT10, PT11))

        b.wait("tensor", xln2)
        for hp in range(8):
            if hp + 1 < 8:
                slice_ready[hp + 1] = emit_slice_dma(hp + 1)
            b.wait("sync", *qk_war, *([ffn1] if hp == 0 else []))
            b.op("sync", lambda e, hp=hp: e.dma_start(
                out=WARE[64:KD, 4096:12288], in_=kaug_d[hp]), inc="DKA", n=16)
            kaug_mark = b.mark("DKA")

            b.wait("tensor", slice_ready[hp], *qk_war,
                   *exp_free, *pw_free[0], *pw_free[1])
            kevac = []
            for widx in (2, 3):
                j = widx - 2
                for half in range(2):
                    for cb in range(CB):
                        for nch in range(2):
                            b.op("tensor", lambda e, widx=widx, cb=cb, hp=hp,
                                 half=half, nch=nch: e.matmul(
                                     PW[half][:, nch * 512:nch * 512 + 512],
                                     slice_view(hp, widx, cb),
                                     XLN[:, cb * 2048 + half * 1024 + nch * 512:
                                         cb * 2048 + half * 1024 + nch * 512 + 512],
                                     start=(cb == 0), stop=(cb == 7)), inc="T")
                tmk = b.mark("T")
                for hh in range(2):
                    for half in range(2):
                        eng = "vector" if (hh + half) % 2 == 0 else "scalar"
                        b.wait(eng, tmk)
                        dst = ktile(j, hh)[0:64, half * 1024:half * 1024 + 1024]
                        src = PW[half][64 * hh:64 * hh + 64, 0:1024]
                        if eng == "vector":
                            b.op("vector", lambda e, d=dst, s=src:
                                 e.tensor_copy(d, s), inc="V")
                        else:
                            b.op("scalar", lambda e, d=dst, s=src:
                                 e.copy(d, s), inc="S")
                kevac += [b.mark("V"), b.mark("S")]
                b.wait("tensor", *kevac)
            b.wait("tensor", *vtm_war)
            vp_evac = {}
            for tt in range(16):
                par = tt % 2
                if tt >= 2:
                    b.wait("tensor", vp_evac[tt - 2])
                for cb in range(CB):
                    b.op("tensor", lambda e, tt=tt, cb=cb, hp=hp, par=par:
                         e.matmul(
                             PW[par][:, 0:128],
                             XLN[:, cb * 2048 + tt * 128:
                                 cb * 2048 + tt * 128 + 128],
                             slice_view(hp, 4, cb),
                             start=(cb == 0), stop=(cb == 7)), inc="T")
                tmk = b.mark("T")
                b.wait("vector", tmk)
                b.op("vector", lambda e, tt=tt, par=par: e.tensor_copy(
                    bass.AP(VTM, tt * 130, [[2080, 128], [65, 2], [1, 64]]),
                    PW[par][:, 0:128].rearrange("p (h c) -> p h c", h=2)),
                    inc="V")
                vp_evac[tt] = b.mark("V")
            vevac = [vp_evac[14], vp_evac[15]]

            for qh in range(2):
                b.wait("sync", *qk_war, *([ffn1] if hp == 0 else []))
                b.op("sync", lambda e, hp=hp, qh=qh: e.dma_start(
                    out=WARE[64:KD, 0:4096], in_=qaug_d[hp, qh]),
                    inc="DQA", n=16)
                qaug_mark = b.mark("DQA")
                b.wait("tensor", *qk_war, *exp_free, *vevac)
                for widx in (0, 1):
                    for cb in range(CB):
                        for nch in range(2):
                            b.op("tensor", lambda e, widx=widx, cb=cb, hp=hp,
                                 qh=qh, nch=nch: e.matmul(
                                     PW[widx][:, nch * 512:nch * 512 + 512],
                                     slice_view(hp, widx, cb),
                                     XLN[:, cb * 2048 + qh * 1024 + nch * 512:
                                         cb * 2048 + qh * 1024 + nch * 512 + 512],
                                     start=(cb == 0), stop=(cb == 7)), inc="T")
                tmk = b.mark("T")
                qevac = []
                for widx in (0, 1):
                    for hh in range(2):
                        eng = "vector" if hh == 0 else "scalar"
                        b.wait(eng, tmk)
                        dst = qtile(widx, hh)[0:64, :]
                        src = PW[widx][64 * hh:64 * hh + 64, 0:1024]
                        if eng == "vector":
                            b.op("vector", lambda e, d=dst, s=src:
                                 e.tensor_copy(d, s), inc="V")
                        else:
                            b.op("scalar", lambda e, d=dst, s=src:
                                 e.copy(d, s), inc="S")
                qevac = [b.mark("V"), b.mark("S")]
                if _DUMP == "proj0" and hp == 0 and qh == 0:
                    b.wait("gpsimd", *qevac, *kevac, *vevac,
                           kaug_mark, qaug_mark)
                    finish_dump(b.mark("P"), WARE[:, 0:16384], True)
                    return nc

                for hh in range(2):
                    DK = DTAB[2 * hp + hh]
                    ktmax = 8 if qh == 0 else 16
                    kt_lo = max(0, qh * 8 - DK + 1)
                    kt7 = max(0, qh * 8 + 8 - DK)
                    exp_done = {}
                    b.wait("tensor", kaug_mark, qaug_mark,
                           *qevac, *kevac, *vevac)

                    def emit_pv(j, kt, DK=DK, kt_lo=kt_lo, kt7=kt7,
                                ktmax=ktmax, hh=hh, qh=qh):
                        qc0 = max(0, kt * 128 - qh * 1024)
                        qt_hi = min(8, kt + DK - qh * 8)
                        b.wait("tensor", exp_done[(j, kt)])
                        if kt == kt_lo and j == 0:
                            b.wait("tensor", *po_free)
                        for qt in range(qc0 // 128, qt_hi):
                            dest = (PO[j][:, qt * 65:qt * 65 + 65]
                                    if qt < 7
                                    else PR1[:, j * 65:j * 65 + 65])
                            if qt < 7:
                                mm_start = (kt == kt_lo and qt == 0)
                                mm_stop = (kt == ktmax - 2 and qt == 6)
                            else:
                                mm_start = (kt == kt7 and j == 0)
                                mm_stop = (kt == ktmax - 1 and j == 1)
                            b.op("tensor", lambda e, qt=qt, kt=kt,
                                 hh=hh, dest=dest, j=j,
                                 mm_start=mm_start, mm_stop=mm_stop:
                                 e.matmul(
                                     dest,
                                     PTJ[j][kt % 2][:, qt * 128:qt * 128 + 128],
                                     VTM[:, kt * 130 + hh * 65:
                                         kt * 130 + hh * 65 + 65],
                                     start=mm_start, stop=mm_stop),
                                 inc="T")

                    for kt in range(kt_lo, ktmax):
                        qc0 = max(0, kt * 128 - qh * 1024)
                        qc_hi = min(1024, (kt + DK) * 128 - qh * 1024)
                        has_mask = kt * 128 >= qh * 1024
                        pieces = []
                        if qc0 < 512:
                            pieces.append((qc0, min(512, qc_hi),
                                           True, not has_mask))
                            if qc_hi > 512:
                                pieces.append((512, qc_hi, True, True))
                        else:
                            pieces.append((qc0, qc_hi, True, not has_mask))
                        for j in range(2):
                            b.wait("tensor", exp_free[j])
                            for (a0, a1, st, sp) in pieces:
                                b.op("tensor", lambda e, j=j, hh=hh, kt=kt,
                                     a0=a0, a1=a1, st=st, sp=sp:
                                     e.matmul(
                                         PW[j][:, a0:a1],
                                         ktile(j, hh)[:,
                                                      kt * 128:kt * 128 + 128],
                                         qtile(j, hh)[:, a0:a1],
                                         start=st, stop=sp), inc="T")
                            if has_mask:
                                b.op("tensor", lambda e, j=j, qc0=qc0:
                                     e.matmul(PW[j][:, qc0:qc0 + 128],
                                              IDEN[:], MASKT[:],
                                              start=False, stop=True), inc="T")
                            st_mark = b.mark("T")
                            b.wait("scalar", st_mark, pt_free[(j, kt % 2)])
                            b.op("scalar", lambda e, j=j, qc0=qc0, kt=kt,
                                 qc_hi=qc_hi: e.activation(
                                     PTJ[j][kt % 2][:, qc0:qc_hi],
                                     PW[j][:, qc0:qc_hi], AF.Exp,
                                     scale=SCALE), inc="S")
                            exp_done[(j, kt)] = b.mark("S")
                            exp_free[j] = exp_done[(j, kt)]
                        if kt > kt_lo:
                            emit_pv(0, kt - 1)
                            pt_free[(0, (kt - 1) % 2)] = b.mark("T")
                            emit_pv(1, kt - 1)
                            pt_free[(1, (kt - 1) % 2)] = b.mark("T")
                    emit_pv(0, ktmax - 1)
                    pt_free[(0, (ktmax - 1) % 2)] = b.mark("T")
                    emit_pv(1, ktmax - 1)
                    pt_free[(1, (ktmax - 1) % 2)] = b.mark("T")
                    pv_done = b.mark("T")

                    h = 2 * hp + hh
                    b.wait("vector", pv_done)

                    def ve(fn):
                        b.op("vector", fn, inc="V")
                        b.wait("vector", b.mark("V"))

                    def r_b(c0, n):
                        return bass.AP(RECIPS, c0, [[32, 128], [1, n], [0, 64]])

                    def mu_b(c0, n):
                        return bass.AP(MUVAR, c0, [[32, 128], [1, n], [0, 64]])
                    CSB_ = TMP[:, 0:512]
                    T2SB_ = TMP[:, 512:1024]
                    csb_v = bass.AP(TMP, 0, [[2048, 128], [64, 8], [1, 64]])
                    t2_v = bass.AP(TMP, 512, [[2048, 128], [64, 8], [1, 64]])
                    ve(lambda e: e.tensor_copy(ROWS[:, 0:455], PO1[:, 0:455]))
                    ve(lambda e: e.tensor_copy(ROWS[:, 520:975], PO2[:, 0:455]))
                    ve(lambda e: e.tensor_copy(
                        bass.AP(ROWS, 455, [[2048, 128], [520, 2], [1, 65]]),
                        PR1[:, 0:130].rearrange("p (j c) -> p j c", j=2)))
                    po_free = [b.mark("V")]
                    ve(lambda e: e.reciprocal(
                        RECIPS[:, 0:7], bass.AP(ROWS, 64, [[2048, 128], [65, 7]])))
                    ve(lambda e: e.reciprocal(RECIPS[:, 7:8], ROWS[:, 519:520]))
                    ve(lambda e: e.reciprocal(
                        RECIPS[:, 8:15],
                        bass.AP(ROWS, 584, [[2048, 128], [65, 7]])))
                    ve(lambda e: e.reciprocal(RECIPS[:, 15:16],
                                              ROWS[:, 1039:1040]))
                    ve(lambda e: e.tensor_scalar_mul(
                        RECIPS[:, 16:24], RECIPS[:, 8:16], NLAM[:, 0:1]))
                    ve(lambda e: e.tensor_tensor(
                        csb_v[:, 0:7, :],
                        bass.AP(ROWS, 0, [[2048, 128], [65, 7], [1, 64]]),
                        r_b(0, 7), ALU.mult))
                    ve(lambda e: e.tensor_tensor(
                        csb_v[:, 7:8, :],
                        bass.AP(ROWS, 455, [[2048, 128], [65, 1], [1, 64]]),
                        r_b(7, 1), ALU.mult))
                    ve(lambda e: e.tensor_tensor(
                        t2_v[:, 0:7, :],
                        bass.AP(ROWS, 520, [[2048, 128], [65, 7], [1, 64]]),
                        r_b(16, 7), ALU.mult))
                    ve(lambda e: e.tensor_tensor(
                        t2_v[:, 7:8, :],
                        bass.AP(ROWS, 975, [[2048, 128], [65, 1], [1, 64]]),
                        r_b(23, 1), ALU.mult))
                    ve(lambda e: e.tensor_add(CSB_, CSB_, T2SB_))
                    ve(lambda e: e.reduce_sum(
                        MUVAR[:, 0:8].rearrange("p (t o) -> p t o", t=8),
                        csb_v, axis=AX.X))
                    ve(lambda e: e.tensor_scalar_mul(
                        MUVAR[:, 0:8], MUVAR[:, 0:8], 1.0 / DH))
                    ve(lambda e: e.tensor_tensor(
                        t2_v, csb_v, mu_b(0, 8), ALU.subtract))
                    ve(lambda e: e.tensor_mul(CSB_, T2SB_, T2SB_))
                    ve(lambda e: e.reduce_sum(
                        MUVAR[:, 8:16].rearrange("p (t o) -> p t o", t=8),
                        csb_v, axis=AX.X))
                    vgn = b.mark("V")
                    b.wait("scalar", vgn, ("P", 2))
                    b.op("scalar", lambda e: e.activation(
                        MUVAR[:, 16:24], MUVAR[:, 8:16], AF.Ln,
                        bias=EPSC[:], scale=1.0 / DH), inc="S")
                    b.wait("scalar", b.mark("S"))
                    b.op("scalar", lambda e: e.activation(
                        MUVAR[:, 8:16], MUVAR[:, 16:24], AF.Exp, scale=-0.5),
                        inc="S")
                    sgn = b.mark("S")
                    b.wait("vector", sgn)
                    b.op("vector", lambda e, h=h, qh=qh: e.tensor_tensor(
                        bass.AP(A2, (qh * 8) * 1024 + h * 64,
                                [[16384, 128], [1024, 8], [1, 64]]),
                        t2_v, mu_b(8, 8), ALU.mult), inc="V")
                qk_war = [b.mark("T")]
            vtm_war = [b.mark("T")]
            slice_war[hp % 2] = [b.mark("T")]
        attn_core_V = b.mark("V")
        attn_core_T = b.mark("T")
        if _DUMP == "attncore":
            finish_dump(attn_core_V, A2[:], True)
            return nc

        b.wait("gpsimd", *slice_war[0], *slice_war[1])
        b.op("gpsimd", lambda e: e.dma_start(
            out=WARE[:, WO_OFF:WO_OFF + 8192].rearrange(
                "p (eb d) -> p eb d", eb=CB),
            in_=wo_d.rearrange("(eb p) d -> p eb d", p=128)), inc="DW", n=16)
        dg_wo = b.mark("DW")
        b.wait("tensor", dg_wo, dg_lnb, attn_core_T, *po_free)
        for db in range(CB):
            for eb in range(CB):
                b.op("tensor", lambda e, db=db, eb=eb: e.matmul(
                    PR1[:, db:db + 1],
                    WARE[:, WO_OFF + eb * 1024 + db * 128:
                         WO_OFF + eb * 1024 + db * 128 + 128],
                    LNBC[:, eb:eb + 1],
                    start=(eb == 0 and db == 0),
                    stop=(eb == 7 and db == 7)), inc="T")
        tbias = b.mark("T")
        b.wait("vector", tbias)
        b.op("vector", lambda e: e.tensor_copy(BIASD[:], PR1[:, 0:8]), inc="V")
        vbias = b.mark("V")
        b.wait("vector", tbias)
        for eb in range(CB):
            b.op("vector", lambda e, eb=eb: e.tensor_scalar_mul(
                WARE[:, WO_OFF + eb * 1024:WO_OFF + eb * 1024 + 1024],
                WARE[:, WO_OFF + eb * 1024:WO_OFF + eb * 1024 + 1024],
                LNWC[:, eb:eb + 1]), inc="V")
        p_wo = b.mark("V")
        b.wait("tensor", attn_core_V)
        tr_evac = {}
        for r in range(32):
            if r >= 1:
                b.wait("tensor", tr_evac[r - 1])
            for s in range(4):
                idx = r * 4 + s
                tt, eb = idx // 8, idx % 8
                b.op("tensor", lambda e, tt=tt, eb=eb, s=s: e.transpose(
                    PTR[:, s * 128:s * 128 + 128],
                    A2[:, tt * 1024 + eb * 128: tt * 1024 + eb * 128 + 128],
                    IDEN[:]), inc="T")
            tmk = b.mark("T")
            b.wait("vector", tmk)
            for s in range(4):
                idx = r * 4 + s
                tt, eb = idx // 8, idx % 8
                b.op("vector", lambda e, tt=tt, eb=eb, s=s: e.tensor_copy(
                    XLN[:, eb * 2048 + tt * 128: eb * 2048 + tt * 128 + 128],
                    PTR[:, s * 128:s * 128 + 128]), inc="V")
            tr_evac[r] = b.mark("V")
        tr_done = b.mark("V")
        b.wait("tensor", tr_done, p_wo)
        b.wait("vector", vbias)
        wo_evac = {}
        for db in range(CB):
            for half in range(2):
                k = db * 2 + half
                par = k % 2
                if k >= 2:
                    b.wait("tensor", wo_evac[k - 2])
                for eb in range(CB):
                    for nch in range(2):
                        b.op("tensor", lambda e, db=db, eb=eb, half=half,
                             nch=nch, par=par: e.matmul(
                                 PW[par][:, nch * 512:nch * 512 + 512],
                                 WARE[:, WO_OFF + eb * 1024 + db * 128:
                                      WO_OFF + eb * 1024 + db * 128 + 128],
                                 XLN[:, eb * 2048 + half * 1024 + nch * 512:
                                     eb * 2048 + half * 1024 + nch * 512 + 512],
                                 start=(eb == 0), stop=(eb == 7)), inc="T")
                tmk = b.mark("T")
                b.wait("vector", tmk)
                b.op("vector", lambda e, db=db, half=half, par=par:
                     e.scalar_tensor_tensor(
                         R[:, db * 2048 + half * 1024:
                           db * 2048 + half * 1024 + 1024],
                         R[:, db * 2048 + half * 1024:
                           db * 2048 + half * 1024 + 1024],
                         1.0, PW[par][:, 0:1024], ALU.mult, ALU.add), inc="V")
                b.wait("vector", b.mark("V"))
                b.op("vector", lambda e, db=db, half=half: e.tensor_scalar_add(
                    R[:, db * 2048 + half * 1024:
                      db * 2048 + half * 1024 + 1024],
                    R[:, db * 2048 + half * 1024:
                      db * 2048 + half * 1024 + 1024],
                    BIASD[:, db:db + 1]), inc="V")
                wo_evac[k] = b.mark("V")
        pw_free[0] = [wo_evac[14]]
        pw_free[1] = [wo_evac[15]]
        attn_done = b.mark("V")

        if _DUMP == "attn":
            finish_dump(attn_done, R[:], False)
            return nc
        b.wait("gpsimd", attn_done)
        for off, src in ((WK_OFF, wkt_d), (WV_OFF, wvt_d), (WR_OFF, wrt_d)):
            b.op("gpsimd", lambda e, o=off, s=src: e.dma_start(
                out=WARE[:, o:o + 8192].rearrange("p (cb o) -> p cb o", cb=CB),
                in_=s.rearrange("(cb p) o -> p cb o", p=128)), inc="DW", n=16)
        dg_ffnw2 = b.mark("DW")
        cast3 = emit_cast(attn_done, [b.mark("T")])
        xln3 = emit_ln([cast3] * 4, [attn_done])
        if _DUMP == "xln3":
            finish_dump(xln3, XLN[:], True)
            return nc
        ffn2 = emit_ffn(dg_ffnw2, xln3, tr_done)

        b.wait("sync", ffn2)
        b.op("sync", lambda e: e.dma_start(
            out=y_d.rearrange("(cb p) t -> p cb t", p=128),
            in_=R[:].rearrange("p (cb t) -> p cb t", cb=CB)), inc="DX", n=16)
        b.wait("sync", "DX")

        with nc.Block() as block:
            b.emit(block)
    return nc


def _host_inputs(ins):
    f32 = np.float32
    sh = {}
    for name, key in (("wq0", "L0_attn_wq0"), ("wq1", "L0_attn_wq1"),
                      ("wk0", "L0_attn_wk0"), ("wk1", "L0_attn_wk1"),
                      ("wv", "L0_attn_wv")):
        w = np.asarray(ins[key], f32)
        sh[name] = np.ascontiguousarray(
            w.reshape(D, 8, 128).transpose(1, 0, 2))
    sh["wo"] = np.ascontiguousarray(np.asarray(ins["L0_attn_wo"], f32))
    sh["wrt"] = np.ascontiguousarray(np.asarray(ins["L0_ffn_W_r"], f32).T)
    sh["wkt"] = np.ascontiguousarray(np.asarray(ins["L0_ffn_W_k"], f32).T)
    sh["wvt"] = np.ascontiguousarray(np.asarray(ins["L0_ffn_W_v"], f32).T)
    sh["mur"] = np.ascontiguousarray(
        np.asarray(ins["L0_ffn_mu_r"], f32).reshape(8, 128).T)
    sh["muk"] = np.ascontiguousarray(
        np.asarray(ins["L0_ffn_mu_k"], f32).reshape(8, 128).T)
    sh["lnw"] = np.ascontiguousarray(
        np.asarray(ins["L0_attn_ln_w"], f32).reshape(-1).reshape(8, 128).T)
    sh["lnb"] = np.ascontiguousarray(
        np.asarray(ins["L0_attn_ln_b"], f32).reshape(-1).reshape(8, 128).T)
    lv = np.stack([np.asarray(ins["L0_attn_l0"], f32),
                   np.asarray(ins["L0_attn_l1"], f32),
                   np.asarray(ins["L0_attn_l2"], f32),
                   np.asarray(ins["L0_attn_l3"], f32)], axis=1)
    sh["lvec"] = np.ascontiguousarray(lv)

    kk, qq = np.meshgrid(np.arange(128), np.arange(128), indexing="ij")
    sh["maskt"] = np.where(kk > qq, -1e30, 0.0).astype(BF)
    sh["iden"] = np.eye(128).astype(BF)
    sh["onesc"] = np.ones((128, 1), BF)
    sh["onesf"] = np.ones((65, 128), f32)
    slopes = _slopes()
    t_idx = np.arange(T, dtype=np.float64)
    kaug = np.zeros((8, 7, 8192), np.float64)
    for hp in range(8):
        for blk in range(4):
            h = 2 * hp + blk // 2
            ks = _bf16_splits(8.0 * slopes[h] * t_idx, NKR)
            kaug[hp, 0:NKR, blk * 2048:(blk + 1) * 2048] = ks
            kaug[hp, NKR:7, blk * 2048:(blk + 1) * 2048] = 1.0
    sh["kaug"] = kaug.astype(BF)
    qaug = np.zeros((8, 2, 7, 4096), np.float64)
    for hp in range(8):
        for qh in range(2):
            for blk in range(4):
                h = 2 * hp + blk // 2
                qs = _bf16_splits(
                    -8.0 * slopes[h] * (qh * 1024 + np.arange(1024)), NQR)
                qaug[hp, qh, 0:NKR, blk * 1024:(blk + 1) * 1024] = 1.0
                qaug[hp, qh, NKR:7, blk * 1024:(blk + 1) * 1024] = qs
    sh["qaug"] = qaug.astype(BF)
    return sh


def kernel(**inputs):
    ins = {k: np.asarray(v) for k, v in inputs.items()}
    sh = _host_inputs(ins)
    x = np.asarray(ins["L0_attn_inputs"], np.float32)
    nc = build_nc()
    in_maps = []
    for bidx in range(8):
        m = dict(sh)
        m["x"] = np.ascontiguousarray(x[bidx].T)
        in_maps.append(m)
    res = run_bass_kernel_spmd(nc, in_maps, core_ids=list(range(8)))
    return np.stack([np.asarray(res.results[i]["y"]).T for i in range(8)])


if __name__ == "__main__":
    import reference as ref
    ins = ref.setup_inputs()
    out = kernel(**{k: np.asarray(v) for k, v in ins.items()})
    expect = np.asarray(ref.reference(**ins))
    err = np.abs(out - expect)
    rel = np.sqrt(((out - expect) ** 2).mean() / (expect ** 2).mean())
    print("rel_l2", rel, "max_abs", err.max())
